# version 01 (final)
# baseline (speedup 1.0000x reference)
import numpy as np
import concourse.bass as bass
import concourse.mybir as mybir
from concourse.bass_utils import run_bass_kernel_spmd
from concourse.bass import IndirectOffsetOnAxis
from contextlib import ExitStack

F32 = mybir.dt.float32
BF16 = mybir.dt.bfloat16
FP16 = mybir.dt.float16
U32 = mybir.dt.uint32
AF = mybir.ActivationFunctionType
ALU = mybir.AluOpType
AX = mybir.AxisListType

NCORES = 8
TPC = 2048
SEQ = 4096
D = 1024
DCH = 8
NH = 16
HD = 64
NEXP = 8
CAP = 5120
QT = 512
NQT = TPC // QT

ENGINES = ("sync", "gpsimd", "tensor", "vector", "scalar")


class Prog:

    def __init__(self, nc, es):
        self.nc = nc
        self.q = {e: [] for e in ENGINES}
        self.sem = {}
        self.cnt = {}
        for name in ("dS", "dG", "tT", "tV", "tC", "dO"):
            self.sem[name] = es.enter_context(nc.semaphore(name))
            self.cnt[name] = 0

    def emit(self, eng, *, wait=None, inc=None, inc_by=1):
        waits = {k: v for k, v in (wait or {}).items() if v and v > 0}
        if inc is not None:
            self.cnt[inc] += inc_by

        def deco(fn):
            def run(engine, _w=waits, _fn=fn, _inc=inc, _by=inc_by):
                for s, v in _w.items():
                    engine.wait_ge(self.sem[s], v)
                ins = _fn(engine)
                if _inc is not None:
                    assert ins is not None, f"emitter at {_fn.__code__.co_filename}:{_fn.__code__.co_firstlineno} returned None"
                    ins.then_inc(self.sem[_inc], _by)
            self.q[eng].append(run)
            return fn
        return deco

    def now(self, *names):
        if len(names) == 1:
            return self.cnt[names[0]]
        return {n: self.cnt[n] for n in names}

    def build(self, block):
        def mk(ops):
            def runner(engine):
                for op in ops:
                    op(engine)
            return runner
        for eng in ENGINES:
            if self.q[eng]:
                getattr(block, eng)(mk(self.q[eng]))


def emit_ln_rows(prog, p_s1, p_s2, murow, rsrow, tmprow, n, eps):
    E = prog.emit
    tT = prog.now("tT")

    @E("vector", wait={"tT": tT}, inc="tV")
    def _(v):
        return v.tensor_scalar_mul(murow[:], p_s1[:], 1.0 / n)

    @E("vector", inc="tV")
    def _(v):
        return v.tensor_scalar_mul(tmprow[:], p_s2[:], 1.0 / n)

    @E("vector", inc="tV")
    def _(v):
        return v.tensor_mul(rsrow[:], murow[:], murow[:])

    @E("vector", inc="tV")
    def _(v):
        return v.tensor_sub(tmprow[:], tmprow[:], rsrow[:])

    @E("vector", inc="tV")
    def _(v):
        return v.tensor_scalar_add(tmprow[:], tmprow[:], eps)

    @E("scalar", wait={"tV": prog.now("tV")}, inc="tC")
    def _(s):
        return s.sqrt(tmprow[:], tmprow[:])

    @E("vector", wait={"tC": prog.now("tC")}, inc="tV")
    def _(v):
        return v.reciprocal(rsrow[:], tmprow[:])


def build_launch1(dbg=False, maxphase=9):
    nc = bass.Bass("TRN2", target_bir_lowering=False, debug=False)
    xT = nc.dram_tensor("xT", [D, SEQ], F32, kind="ExternalInput").ap()
    wqkv = nc.dram_tensor("wqkv", [3, D, D], F32, kind="ExternalInput").ap()
    wo = nc.dram_tensor("wo", [D, D], F32, kind="ExternalInput").ap()
    gam = nc.dram_tensor("gam", [128, DCH], F32, kind="ExternalInput").ap()
    bet = nc.dram_tensor("bet", [128, DCH], F32, kind="ExternalInput").ap()
    g2 = nc.dram_tensor("g2", [128, DCH], F32, kind="ExternalInput").ap()
    b2 = nc.dram_tensor("b2", [128, DCH], F32, kind="ExternalInput").ap()
    routw = nc.dram_tensor("routw", [D, NEXP], F32, kind="ExternalInput").ap()
    thr = nc.dram_tensor("thr", [1, TPC], F32, kind="ExternalInput").ap()
    ckey = nc.dram_tensor("ckey", [128, NQT * (SEQ // 128)], F32, kind="ExternalInput").ap()
    iden = nc.dram_tensor("iden", [128, 128], BF16, kind="ExternalInput").ap()
    ut128 = nc.dram_tensor("ut128", [128, 128], BF16, kind="ExternalInput").ap()
    iota = nc.dram_tensor("iota", [128, 1], F32, kind="ExternalInput").ap()
    i8m = nc.dram_tensor("i8m", [128, NEXP], F32, kind="ExternalInput").ap()
    mcrow = nc.dram_tensor("mcrow", [128, 128], BF16, kind="ExternalInput").ap()
    mscsr = nc.dram_tensor("mscsr", [128, 2 * NEXP], BF16, kind="ExternalInput").ap()
    idenf = nc.dram_tensor("idenf", [128, 128], F32, kind="ExternalInput").ap()
    qpar = nc.dram_tensor("qpar", [1, 1], U32, kind="ExternalInput").ap()

    xnd = nc.dram_tensor("xnd", [D, SEQ], BF16, kind=("ExternalOutput" if (dbg or maxphase < 9) else "Internal")).ap()
    x1o = nc.dram_tensor("x1o", [D, TPC], F32, kind="ExternalOutput").ap()
    compo = nc.dram_tensor("compo", [2 * TPC, D], BF16, kind="ExternalOutput").ap()
    rmeta = nc.dram_tensor("rmeta", [128, 16 * 6], F32, kind="ExternalOutput").ap()
    cnt8 = nc.dram_tensor("cnt8", [1, NEXP], F32, kind="ExternalOutput").ap()

    es = ExitStack()
    sb = lambda name, shape, dt: es.enter_context(nc.sbuf_tensor(name, shape, dt))
    psu = lambda name, shape, dt=F32: es.enter_context(nc.psum_tensor(name, shape, dt))

    xn = sb("xn", [128, DCH, SEQ], BF16)
    attn = sb("attn", [128, DCH, TPC], BF16)
    xnq = sb("xnq", [128, 2, DCH, QT], BF16)
    c_iden = sb("c_iden", [128, 128], BF16)
    c_ut = sb("c_ut", [128, 128], BF16)
    c_iota = sb("c_iota", [128, 1], F32)
    c_i8m = sb("c_i8m", [128, NEXP], F32)
    c_mcrow = sb("c_mcrow", [128, 128], BF16)
    c_mscsr = sb("c_mscsr", [128, 2 * NEXP], BF16)
    c_idenf = sb("c_idenf", [128, 128], F32)
    c_thr = sb("c_thr", [1, TPC], F32)
    c_ckey = sb("c_ckey", [128, NQT, SEQ // 128], F32)
    c_gam = sb("c_gam", [128, DCH], F32)
    c_bet = sb("c_bet", [128, DCH], F32)
    c_g2 = sb("c_g2", [128, DCH], F32)
    c_b2 = sb("c_b2", [128, DCH], F32)
    c_routw = sb("c_routw", [128, DCH, NEXP], BF16)
    onesb = sb("onesb", [128, 1], BF16)
    onesf = sb("onesf", [1, 128], F32)
    onesfc = sb("onesfc", [128, 1], F32)
    c_qpar = sb("c_qpar", [1, 1], U32)
    iotk = sb("iotk", [128, 32], F32)
    murow = sb("murow", [1, QT], F32)
    rsrow = sb("rsrow", [1, QT], F32)
    tmprow = sb("tmprow", [1, QT], F32)
    zrow = sb("zrow", [1, QT], F32)
    mub = sb("mub", [128, QT], F32)
    rstdb = sb("rstdb", [128, QT], F32)
    p_s1 = psu("p_s1", [1, QT])
    p_s2 = psu("p_s2", [1, QT])
    p_bc = psu("p_bc", [128, QT])
    p_bc2 = psu("p_bc2", [128, QT])
    p_mm = psu("p_mm", [128, QT])
    p_mm2 = psu("p_mm2", [128, QT])
    p_pv = psu("p_pv", [128, QT])
    pt_b = es.enter_context(nc.psum_tensor("pt_b", [128, 128], BF16))
    p_z = p_s1

    esP = ExitStack()
    sbp = lambda name, shape, dt: esP.enter_context(nc.sbuf_tensor(name, shape, dt))
    xf = sbp("xf", [128, 2, DCH, QT], F32)
    xsq = sbp("xsq", [128, DCH, QT], BF16)
    xb16 = sbp("xb16", [128, DCH, QT], BF16)

    prog = Prog(nc, es)
    E = prog.emit

    for dst, src in [(c_iden, iden[:, :]), (c_ut, ut128[:, :]), (c_iota, iota[:, :]),
                     (c_i8m, i8m[:, :]), (c_mcrow, mcrow[:, :]), (c_mscsr, mscsr[:, :]), (c_idenf, idenf[:, :]), (c_thr, thr[:, :]), (c_ckey.rearrange("p j k -> p (j k)"), ckey[:, :]),
                     (c_gam, gam[:, :]), (c_bet, bet[:, :]), (c_g2, g2[:, :]),
                     (c_b2, b2[:, :])]:
        @E("sync", inc="dS", inc_by=16)
        def _(s, dst=dst, src=src):
            return s.dma_start(out=dst[:], in_=src)

    @E("gpsimd", inc="dG", inc_by=16)
    def _(g):
        return g.dma_start(out=c_routw[:, :, :],
                           in_=routw.rearrange("(c p) e -> p c e", p=128))

    @E("vector", inc="tV")
    def _(v):
        return v.memset(onesb[:], 1.0)

    @E("vector", inc="tV")
    def _(v):
        return v.memset(onesf[:], 1.0)

    @E("vector", inc="tV")
    def _(v):
        return v.memset(onesfc[:], 1.0)

    @E("sync", inc="dS", inc_by=16)
    def _(s):
        return s.dma_start(out=c_qpar[:], in_=qpar[:, :])

    for kc in range(32):
        @E("vector", wait={"dS": prog.now("dS")} if kc == 0 else None, inc="tV")
        def _(v, kc=kc):
            return v.tensor_scalar_add(iotk[:, kc:kc + 1], c_iota[:], float(kc * 128))
    CONSTS = prog.now("dS", "dG", "tV")

    NT = SEQ // QT
    tv_tile_done = [0] * NT
    for t in range(NT):
        buf = t % 2
        w = {"tV": tv_tile_done[t - 2]} if t >= 2 else {}
        @E("sync", wait=w, inc="dS", inc_by=16)
        def _(s, t=t, buf=buf):
            return s.dma_start(out=xf[:, buf, :, :],
                               in_=xT.rearrange("(c p) (n q) -> p n c q", p=128, q=QT)[:, t, :, :])
        @E("vector", wait={**({} if t else CONSTS), "dS": prog.now("dS"),
                           "tC": prog.now("tC"), "tT": prog.now("tT")}, inc="tV")
        def _(v, buf=buf):
            return v.tensor_copy(xb16[:].rearrange("p c q -> p (c q)"),
                                 xf[:, buf].rearrange("p c q -> p (c q)"))
        @E("scalar", wait={"tV": prog.now("tV")}, inc="tC")
        def _(s):
            return s.activation(xsq[:].rearrange("p c q -> p (c q)"),
                                xb16[:].rearrange("p c q -> p (c q)"), AF.Square)
        w_mm = {"tV": prog.now("tV"), "tC": prog.now("tC")}
        for dc in range(DCH):
            @E("tensor", wait=(w_mm if dc == 0 else None), inc="tT")
            def _(tn, dc=dc):
                return tn.matmul(p_s1[:], onesb[:], xb16[:, dc, :],
                                 start=(dc == 0), stop=(dc == DCH - 1))
        for dc in range(DCH):
            @E("tensor", inc="tT")
            def _(tn, dc=dc):
                return tn.matmul(p_s2[:], onesb[:], xsq[:, dc, :],
                                 start=(dc == 0), stop=(dc == DCH - 1))
        emit_ln_rows(prog, p_s1, p_s2, murow, rsrow, tmprow, D, 1e-5)
        @E("tensor", wait={"tV": prog.now("tV")}, inc="tT")
        def _(tn):
            return tn.matmul(p_bc[:], onesf[:], murow[:], start=True, stop=True)
        @E("tensor", inc="tT")
        def _(tn):
            return tn.matmul(p_bc2[:], onesf[:], rsrow[:], start=True, stop=True)
        @E("vector", wait={"tT": prog.now("tT")}, inc="tV")
        def _(v):
            return v.tensor_copy(mub[:], p_bc[:])
        @E("vector", inc="tV")
        def _(v):
            return v.tensor_copy(rstdb[:], p_bc2[:])
        for dc in range(DCH):
            @E("vector", wait={"tV": prog.now("tV")} if dc == 0 else None, inc="tV")
            def _(v, dc=dc, buf=buf):
                return v.tensor_sub(xf[:, buf, dc, :], xf[:, buf, dc, :], mub[:])
        for dc in range(DCH):
            @E("vector", inc="tV")
            def _(v, dc=dc, buf=buf, t=t):
                return v.tensor_mul(xn[:, dc, t * QT:(t + 1) * QT], xf[:, buf, dc, :], rstdb[:])
        @E("gpsimd", wait={"tV": prog.now("tV")}, inc="dG", inc_by=16)
        def _(g, t=t):
            return g.dma_start(out=xnd.rearrange("(c p) (n q) -> p n c q", p=128, q=QT)[:, t, :, :],
                               in_=xn[:, :, t * QT:(t + 1) * QT])
        tv_tile_done[t] = prog.now("tV")
    P1 = prog.now("tV")
    P1d = prog.now("dG")
    if maxphase < 2:
        @E("gpsimd")
        def _(g):
            g.wait_ge(prog.sem["dG"], prog.now("dG"))
            g.wait_ge(prog.sem["dO"], prog.now("dO"))
            return None
        esP.close()
        block = es.enter_context(nc.Block())
        prog.build(block)
        es.close()
        return nc
    esP.close()
    esP = ExitStack()
    sbp = lambda name, shape, dt: esP.enter_context(nc.sbuf_tensor(name, shape, dt))
    wq3 = sbp("wq3", [128, 2, 3, DCH, 128], BF16)
    qkp = sbp("qkp", [128, TPC], BF16)
    kkp = sbp("kkp", [128, SEQ], BF16)
    vkp = sbp("vkp", [128, SEQ], BF16)
    vkm = sbp("vkm", [128, SEQ // 128, 128], BF16)
    e1b = sbp("e1b", [128, 32, QT], BF16)
    rzb = rstdb
    thrb = mub

    xn_tiles = xnd.rearrange("(c p) (o q) -> o p c q", p=128, q=QT)
    _dyn = {}

    @E("sync")
    def _(s):
        _dyn["o"] = [s.snap(s.partition_id() % 2 * 4 + t, min_val=0, max_val=7)
                     for t in range(NQT)]
        return None

    def dyn_own_tile(s, dstbuf, t):
        return s.dma_start(out=xnq[:, dstbuf, :, :], in_=xn_tiles[_dyn["o"][t], :, :, :])

    NKC = [4 * j + 20 for j in range(NQT)]

    for hp in range(NH // 2):
        wbuf = hp % 2
        @E("gpsimd", wait={"tT": prog.now("tT")} if hp >= 2 else {"tV": P1}, inc="dG", inc_by=16)
        def _(g, hp=hp, wbuf=wbuf):
            return g.dma_start(
                out=wq3[:, wbuf, :, :, :],
                in_=wqkv.rearrange("i (c p) d -> p i c d", p=128)[:, :, :, hp * 128:(hp + 1) * 128])
        dG_w = prog.now("dG")

        for t in range(NQT):
            qb = t % 2
            @E("sync", wait={"tT": prog.now("tT"), "tV": P1,
                             "dG": P1d}, inc="dS", inc_by=16)
            def _(s, t=t, qb=qb):
                return dyn_own_tile(s, qb, t)
            for dc in range(DCH):
                w = {"dS": prog.now("dS"), "dG": dG_w, "tV": prog.now("tV")} if dc == 0 else None
                @E("tensor", wait=w, inc="tT")
                def _(tn, dc=dc, t=t, qb=qb, wbuf=wbuf):
                    return tn.matmul(p_mm[:], wq3[:, wbuf, 0, dc, :], xnq[:, qb, dc, :],
                                     start=(dc == 0), stop=(dc == DCH - 1))
            @E("vector", wait={"tT": prog.now("tT")}, inc="tV")
            def _(v, t=t):
                return v.tensor_copy(qkp[:, t * QT:(t + 1) * QT], p_mm[:])
        for t in range(SEQ // QT):
            for dc in range(DCH):
                w = {"dG": dG_w, "tV": prog.now("tV")} if dc == 0 else None
                @E("tensor", wait=w, inc="tT")
                def _(tn, dc=dc, t=t, wbuf=wbuf):
                    return tn.matmul(p_mm[:], wq3[:, wbuf, 1, dc, :],
                                     xn[:, dc, t * QT:(t + 1) * QT],
                                     start=(dc == 0), stop=(dc == DCH - 1))
            for dc in range(DCH):
                @E("tensor", inc="tT")
                def _(tn, dc=dc, t=t, wbuf=wbuf):
                    return tn.matmul(p_mm2[:], wq3[:, wbuf, 2, dc, :],
                                     xn[:, dc, t * QT:(t + 1) * QT],
                                     start=(dc == 0), stop=(dc == DCH - 1))
            @E("vector", wait={"tT": prog.now("tT")}, inc="tV")
            def _(v, t=t):
                return v.tensor_copy(kkp[:, t * QT:(t + 1) * QT], p_mm[:])
            @E("vector", inc="tV")
            def _(v, t=t):
                return v.tensor_copy(vkp[:, t * QT:(t + 1) * QT], p_mm2[:])
        tV_proj = prog.now("tV")
        for kb in range(SEQ // 128):
            @E("tensor", wait={"tV": prog.now("tV")}, inc="tT")
            def _(tn, kb=kb):
                return tn.transpose(pt_b[:], vkp[:, kb * 128:(kb + 1) * 128], c_iden[:])
            @E("vector", wait={"tT": prog.now("tT")}, inc="tV")
            def _(v, kb=kb):
                return v.tensor_copy(vkm[:, kb, :], pt_b[:])
        tV_vkm = prog.now("tV")

        for half in range(2):
            h = hp * 2 + half
            rb = half * 64
            for j in range(NQT):
                nkc = NKC[j]
                @E("tensor", wait={"tV": tV_vkm} if (half == 0 and j == 0) else None, inc="tT")
                def _(tn, j=j):
                    return tn.matmul(p_bc[:], onesf[:], c_thr[:, j * QT:(j + 1) * QT],
                                     start=True, stop=True)
                @E("vector", wait={"tT": prog.now("tT")}, inc="tV")
                def _(v):
                    return v.tensor_copy(thrb[:], p_bc[:])
                tV_thr = prog.now("tV")
                for kc in range(nkc):
                    pp = p_mm if kc % 2 == 0 else p_mm2
                    w = {"tC": prog.now("tC")} if kc >= 2 else {"tV": tV_thr}
                    @E("tensor", wait=w, inc="tT")
                    def _(tn, kc=kc, rb=rb, j=j, pp=pp):
                        return tn.matmul(pp[:], kkp[rb:rb + 64, kc * 128:(kc + 1) * 128],
                                         qkp[rb:rb + 64, j * QT:(j + 1) * QT],
                                         start=True, stop=True)
                    @E("scalar", wait={"tT": prog.now("tT")}, inc="tC")
                    def _(s, kc=kc, pp=pp, j=j):
                        return s.activation(e1b[:, kc, :], pp[:], AF.Exp,
                                            bias=c_ckey[:, j, kc:kc + 1], scale=0.125)
                tC_exp = prog.now("tC")
                diag = list(range(4 * j, 4 * j + 4)) + list(range(4 * j + 16, 4 * j + 20))
                for kc in diag:
                    @E("vector", wait={"tC": tC_exp}, inc="tV")
                    def _(v, kc=kc):
                        return v.tensor_scalar(rzb[:], thrb[:], iotk[:, kc:kc + 1], None,
                                               ALU.is_gt)
                    @E("vector", inc="tV")
                    def _(v, kc=kc):
                        return v.tensor_mul(e1b[:, kc, :], e1b[:, kc, :], rzb[:])
                tV_mask = prog.now("tV")
                for kc in range(nkc):
                    @E("tensor", wait={"tV": tV_mask} if kc == 0 else None, inc="tT")
                    def _(tn, kc=kc, nkc=nkc):
                        return tn.matmul(p_z[:], onesb[:], e1b[:, kc, :],
                                         start=(kc == 0), stop=(kc == nkc - 1))
                @E("vector", wait={"tT": prog.now("tT")}, inc="tV")
                def _(v):
                    return v.reciprocal(zrow[:], p_z[:])
                @E("tensor", wait={"tV": prog.now("tV")}, inc="tT")
                def _(tn):
                    return tn.matmul(p_bc2[:], onesf[:], zrow[:], start=True, stop=True)
                @E("vector", wait={"tT": prog.now("tT")}, inc="tV")
                def _(v):
                    return v.tensor_copy(rzb[:], p_bc2[:])
                tV_rz = prog.now("tV")
                for kc in range(nkc):
                    @E("tensor", wait={"tV": tV_rz} if kc == 0 else None, inc="tT")
                    def _(tn, kc=kc, rb=rb, nkc=nkc):
                        return tn.matmul(p_pv[rb:rb + 64, :], vkm[:, kc, rb:rb + 64],
                                         e1b[:, kc, :], start=(kc == 0), stop=(kc == nkc - 1),
                                         tile_position=(0, rb))
                @E("vector", wait={"tT": prog.now("tT")}, inc="tV")
                def _(v, rb=rb, hp=hp, j=j):
                    return v.tensor_mul(attn[rb:rb + 64, hp, j * QT:(j + 1) * QT],
                                        p_pv[rb:rb + 64, :], rzb[rb:rb + 64, :])
    P2 = prog.now("tV")
    if maxphase < 3 and not dbg:
        dbg = True
    if dbg:
        dbg_attn = nc.dram_tensor("dbg_attn", [128, DCH * TPC], BF16, kind="ExternalOutput").ap()
        dbg_q = nc.dram_tensor("dbg_q", [128, TPC], BF16, kind="ExternalOutput").ap()
        dbg_k = nc.dram_tensor("dbg_k", [128, SEQ], BF16, kind="ExternalOutput").ap()
        dbg_v = nc.dram_tensor("dbg_v", [128, SEQ], BF16, kind="ExternalOutput").ap()
        dbg_xnq = nc.dram_tensor("dbg_xnq", [128, DCH * QT], BF16, kind="ExternalOutput").ap()
        @E("gpsimd", wait={"tV": P2}, inc="dO", inc_by=16)
        def _(g):
            return g.dma_start(out=dbg_attn[:, :], in_=attn[:].rearrange("p c q -> p (c q)"))
        @E("gpsimd", inc="dO", inc_by=16)
        def _(g):
            return g.dma_start(out=dbg_q[:, :], in_=qkp[:, :])
        @E("gpsimd", inc="dO", inc_by=16)
        def _(g):
            return g.dma_start(out=dbg_k[:, :], in_=kkp[:, :])
        @E("gpsimd", inc="dO", inc_by=16)
        def _(g):
            return g.dma_start(out=dbg_v[:, :], in_=vkp[:, :])
        @E("gpsimd", inc="dO", inc_by=16)
        def _(g):
            return g.dma_start(out=dbg_xnq[:, :], in_=xnq[:, 0].rearrange("p c q -> p (c q)"))
    esP.close()
    esP = ExitStack()
    sbp = lambda name, shape, dt: esP.enter_context(nc.sbuf_tensor(name, shape, dt))
    wosb = sbp("wosb", [128, DCH, D], BF16)
    ytil = sbp("ytil", [128, DCH, QT], F32)
    ysq = sbp("ysq", [128, DCH, QT], F32)
    xown = sbp("xown", [128, DCH, QT], F32)
    x1til = xown

    @E("gpsimd", wait={"tV": P2}, inc="dG", inc_by=16)
    def _(g):
        return g.dma_start(out=wosb[:, :, :],
                           in_=wo.rearrange("(c p) d -> p c d", p=128))
    dG_wo = prog.now("dG")
    xT_tiles = xT.rearrange("d (o q) -> o d q", q=QT)

    for j in range(NQT):
        @E("sync", wait={"tV": prog.now("tV"), "dO": prog.now("dO")}, inc="dS", inc_by=16)
        def _(s, j=j):
            return s.dma_start(out=xown[:, :, :],
                               in_=xT_tiles.rearrange("o (c p) q -> o p c q", p=128)[_dyn["o"][j], :, :, :])
        @E("sync", inc="dS", inc_by=16)
        def _(s, j=j):
            return dyn_own_tile(s, 0, j)
        dS_j = prog.now("dS")
        if dbg and j == 0:
            dbg_xo = nc.dram_tensor("dbg_xo", [128, DCH * QT], F32, kind="ExternalOutput").ap()
            @E("gpsimd", wait={"dS": dS_j}, inc="dO", inc_by=16)
            def _(g):
                return g.dma_start(out=dbg_xo[:, :], in_=xown[:].rearrange("p c q -> p (c q)"))
        for dc in range(DCH):
            for hc in range(DCH):
                w = {"dG": dG_wo, "dS": dS_j, "tV": prog.now("tV")} if hc == 0 else None
                @E("tensor", wait=w, inc="tT")
                def _(tn, dc=dc, hc=hc, j=j):
                    return tn.matmul(p_mm[:], wosb[:, hc, dc * 128:(dc + 1) * 128],
                                     attn[:, hc, j * QT:(j + 1) * QT],
                                     start=(hc == 0), stop=(hc == DCH - 1))
            @E("vector", wait={"tT": prog.now("tT")}, inc="tV")
            def _(v, dc=dc):
                return v.tensor_add(ytil[:, dc, :], p_mm[:], xnq[:, 0, dc, :])
        if dbg and j == 0:
            dbg_y = nc.dram_tensor("dbg_y", [128, DCH * QT], F32, kind="ExternalOutput").ap()
            @E("gpsimd", wait={"tV": prog.now("tV")}, inc="dO", inc_by=16)
            def _(g):
                return g.dma_start(out=dbg_y[:, :], in_=ytil[:].rearrange("p c q -> p (c q)"))
        @E("scalar", wait={"tV": prog.now("tV")}, inc="tC")
        def _(s, j=j):
            ins = s.activation(ysq[:].rearrange("p c q -> p (c q)"),
                               ytil[:].rearrange("p c q -> p (c q)"), AF.Square)
            if dbg and j == 0:
                import sys
                print("SQUARE INS:", str(ins.ins)[:500], file=sys.stderr)
            return ins
        if dbg and j == 0:
            dbg_xo2 = nc.dram_tensor("dbg_xo2", [128, DCH * QT], F32, kind="ExternalOutput").ap()
            @E("gpsimd", wait={"tC": prog.now("tC")}, inc="dO", inc_by=16)
            def _(g):
                return g.dma_start(out=dbg_xo2[:, :], in_=xown[:].rearrange("p c q -> p (c q)"))
        w_mm = {"tC": prog.now("tC")}
        for dc in range(DCH):
            @E("tensor", wait=(w_mm if dc == 0 else None), inc="tT")
            def _(tn, dc=dc):
                return tn.matmul(p_s1[:], onesfc[:], ytil[:, dc, :],
                                 start=(dc == 0), stop=(dc == DCH - 1))
        for dc in range(DCH):
            @E("tensor", inc="tT")
            def _(tn, dc=dc):
                return tn.matmul(p_s2[:], onesfc[:], ysq[:, dc, :],
                                 start=(dc == 0), stop=(dc == DCH - 1))
        emit_ln_rows(prog, p_s1, p_s2, murow, rsrow, tmprow, D, 1e-3)
        if dbg and j == 0:
            dbg_ln = nc.dram_tensor("dbg_ln", [1, 2 * QT], F32, kind="ExternalOutput").ap()
            @E("gpsimd", wait={"tV": prog.now("tV")}, inc="dO", inc_by=16)
            def _(g):
                return g.dma_start(out=dbg_ln[:, 0:QT], in_=murow[:])
            @E("gpsimd", inc="dO", inc_by=16)
            def _(g):
                return g.dma_start(out=dbg_ln[:, QT:2 * QT], in_=rsrow[:])
        @E("tensor", wait={"tV": prog.now("tV"), "dO": prog.now("dO")}, inc="tT")
        def _(tn):
            return tn.matmul(p_bc[:], onesf[:], murow[:], start=True, stop=True)
        @E("tensor", inc="tT")
        def _(tn):
            return tn.matmul(p_bc2[:], onesf[:], rsrow[:], start=True, stop=True)
        @E("vector", wait={"tT": prog.now("tT")}, inc="tV")
        def _(v):
            return v.tensor_copy(mub[:], p_bc[:])
        @E("vector", inc="tV")
        def _(v):
            return v.tensor_copy(rstdb[:], p_bc2[:])
        for dc in range(DCH):
            @E("vector", inc="tV")
            def _(v, dc=dc):
                return v.tensor_sub(ytil[:, dc, :], ytil[:, dc, :], mub[:])
            @E("vector", inc="tV")
            def _(v, dc=dc):
                return v.tensor_mul(ytil[:, dc, :], ytil[:, dc, :], rstdb[:])
            @E("scalar", wait={"tV": prog.now("tV")}, inc="tC")
            def _(s, dc=dc):
                return s.activation(ytil[:, dc, :], ytil[:, dc, :], AF.Identity,
                                    bias=c_bet[:, dc:dc + 1], scale=c_gam[:, dc:dc + 1])
            @E("vector", wait={"tC": prog.now("tC")}, inc="tV")
            def _(v, dc=dc):
                return v.tensor_add(xown[:, dc, :], xown[:, dc, :], ytil[:, dc, :])
        if dbg and j == 0:
            dbg_x1 = nc.dram_tensor("dbg_x1", [128, DCH * QT], F32, kind="ExternalOutput").ap()
            dbg_y2 = nc.dram_tensor("dbg_y2", [128, DCH * QT], F32, kind="ExternalOutput").ap()
            @E("gpsimd", wait={"tV": prog.now("tV")}, inc="dO", inc_by=16)
            def _(g):
                return g.dma_start(out=dbg_x1[:, :], in_=xown[:].rearrange("p c q -> p (c q)"))
            @E("gpsimd", inc="dO", inc_by=16)
            def _(g):
                return g.dma_start(out=dbg_y2[:, :], in_=ytil[:].rearrange("p c q -> p (c q)"))
        @E("gpsimd", wait={"tV": prog.now("tV")}, inc="dO", inc_by=16)
        def _(g, j=j):
            return g.dma_start(out=x1o.rearrange("(c p) (n q) -> p n c q", p=128, q=QT)[:, j, :, :],
                               in_=xown[:, :, :])
    P3 = prog.now("tV")
    P3dO = prog.now("dO")
    esP.close()
    esP = ExitStack()
    sbp = lambda name, shape, dt: esP.enter_context(nc.sbuf_tensor(name, shape, dt))
    xn2 = sbp("xn2", [128, DCH, TPC], BF16)
    xo4 = sbp("xown4", [128, DCH, QT], F32)
    xpre4 = xo4
    xb16b = sbp("xb16b", [128, DCH, QT], BF16)
    xsqb = sbp("xsqb", [128, DCH, QT], BF16)
    lgt = sbp("lgt", [128, 16, NEXP], F32)
    topv = sbp("topv", [128, 16, NEXP], F32)
    meta = sbp("meta", [128, 16, 6], F32)
    oh1 = sbp("oh1", [128, 16, NEXP], F32)
    oh2 = sbp("oh2", [128, 16, NEXP], F32)
    trow = sbp("trow", [1, 16 * NEXP], F32)
    crow = sbp("crow", [1, 16 * NEXP], F32)
    srow = sbp("srow", [1, NEXP], F32)
    sbc = sbp("sbc", [128, NEXP], F32)
    xtm = sbp("xtm", [128, D], BF16)
    fta = sbp("fta", [128, 16, NEXP], F32)
    ftb = sbp("ftb", [128, 16, NEXP], F32)
    wtmp = sbp("wtmp", [128, 16, 4], F32)
    obca = sbp("obca", [128, 16, NEXP], F32)
    preba = sbp("preba", [128, 16, NEXP], F32)
    slta = sbp("slta", [128, 16, 2], F32)
    sltu = sbp("sltu", [128, 16, 2], U32)
    scro = sbp("scro", [1, NEXP], F32)

    x1o_tiles = x1o.rearrange("(c p) (n q) -> n p c q", p=128, q=QT)
    for j in range(NQT):
        @E("sync", wait={"dO": P3dO, "tC": prog.now("tC")}, inc="dS", inc_by=16)
        def _(s, j=j):
            return s.dma_start(out=xo4[:, :, :], in_=x1o_tiles[j, :, :, :])
        dS_j = prog.now("dS")
        @E("vector", wait={"dS": dS_j}, inc="tV")
        def _(v):
            return v.tensor_copy(xb16b[:].rearrange("p c q -> p (c q)"),
                                 xo4[:].rearrange("p c q -> p (c q)"))
        @E("scalar", wait={"tV": prog.now("tV")}, inc="tC")
        def _(s):
            return s.activation(xsqb[:].rearrange("p c q -> p (c q)"),
                                xb16b[:].rearrange("p c q -> p (c q)"), AF.Square)
        w_mm = {"tV": prog.now("tV"), "tC": prog.now("tC")}
        for dc in range(DCH):
            @E("tensor", wait=(w_mm if dc == 0 else None), inc="tT")
            def _(tn, dc=dc):
                return tn.matmul(p_s1[:], onesb[:], xb16b[:, dc, :],
                                 start=(dc == 0), stop=(dc == DCH - 1))
        for dc in range(DCH):
            @E("tensor", inc="tT")
            def _(tn, dc=dc):
                return tn.matmul(p_s2[:], onesb[:], xsqb[:, dc, :],
                                 start=(dc == 0), stop=(dc == DCH - 1))
        emit_ln_rows(prog, p_s1, p_s2, murow, rsrow, tmprow, D, 1e-5)
        @E("tensor", wait={"tV": prog.now("tV")}, inc="tT")
        def _(tn):
            return tn.matmul(p_bc[:], onesf[:], murow[:], start=True, stop=True)
        @E("tensor", inc="tT")
        def _(tn):
            return tn.matmul(p_bc2[:], onesf[:], rsrow[:], start=True, stop=True)
        @E("vector", wait={"tT": prog.now("tT")}, inc="tV")
        def _(v):
            return v.tensor_copy(mub[:], p_bc[:])
        @E("vector", inc="tV")
        def _(v):
            return v.tensor_copy(rstdb[:], p_bc2[:])
        for dc in range(DCH):
            @E("vector", inc="tV")
            def _(v, dc=dc):
                return v.tensor_sub(xpre4[:, dc, :], xpre4[:, dc, :], mub[:])
            @E("vector", inc="tV")
            def _(v, dc=dc):
                return v.tensor_mul(xpre4[:, dc, :], xpre4[:, dc, :], rstdb[:])
        @E("vector", wait={"tV": prog.now("tV"), "tT": prog.now("tT"),
                           "tC": prog.now("tC")}, inc="tV")
        def _(v):
            return v.tensor_copy(xb16b[:].rearrange("p c q -> p (c q)"),
                                 xpre4[:].rearrange("p c q -> p (c q)"))
        @E("scalar", wait={"tV": prog.now("tV")}, inc="tC")
        def _(s):
            return s.activation(xsqb[:].rearrange("p c q -> p (c q)"),
                                xb16b[:].rearrange("p c q -> p (c q)"), AF.Square)
        w_mm = {"tV": prog.now("tV"), "tC": prog.now("tC")}
        for dc in range(DCH):
            @E("tensor", wait=(w_mm if dc == 0 else None), inc="tT")
            def _(tn, dc=dc):
                return tn.matmul(p_s1[:], onesb[:], xb16b[:, dc, :],
                                 start=(dc == 0), stop=(dc == DCH - 1))
        for dc in range(DCH):
            @E("tensor", inc="tT")
            def _(tn, dc=dc):
                return tn.matmul(p_s2[:], onesb[:], xsqb[:, dc, :],
                                 start=(dc == 0), stop=(dc == DCH - 1))
        emit_ln_rows(prog, p_s1, p_s2, murow, rsrow, tmprow, D, 1e-5)
        @E("tensor", wait={"tV": prog.now("tV")}, inc="tT")
        def _(tn):
            return tn.matmul(p_bc[:], onesf[:], murow[:], start=True, stop=True)
        @E("tensor", inc="tT")
        def _(tn):
            return tn.matmul(p_bc2[:], onesf[:], rsrow[:], start=True, stop=True)
        @E("vector", wait={"tT": prog.now("tT")}, inc="tV")
        def _(v):
            return v.tensor_copy(mub[:], p_bc[:])
        @E("vector", inc="tV")
        def _(v):
            return v.tensor_copy(rstdb[:], p_bc2[:])
        for dc in range(DCH):
            @E("vector", inc="tV")
            def _(v, dc=dc):
                return v.tensor_sub(xpre4[:, dc, :], xpre4[:, dc, :], mub[:])
            @E("vector", inc="tV")
            def _(v, dc=dc):
                return v.tensor_mul(xpre4[:, dc, :], xpre4[:, dc, :], rstdb[:])
            @E("scalar", wait={"tV": prog.now("tV")}, inc="tC")
            def _(s, dc=dc, j=j):
                return s.activation(xn2[:, dc, j * QT:(j + 1) * QT], xpre4[:, dc, :], AF.Identity,
                                    bias=c_b2[:, dc:dc + 1], scale=c_g2[:, dc:dc + 1])
    P4ln = prog.now("tC")

    if dbg:
        dbg_xn2 = nc.dram_tensor("dbg_xn2", [128, DCH * TPC], BF16, kind="ExternalOutput").ap()
        @E("gpsimd", wait={"tC": prog.now("tC")}, inc="dO", inc_by=16)
        def _(g):
            return g.dma_start(out=dbg_xn2[:, :], in_=xn2[:].rearrange("p c q -> p (c q)"))
        dbg_lg = nc.dram_tensor("dbg_lg", [128, 16 * NEXP], F32, kind="ExternalOutput").ap()
    for g16 in range(16):
        for dc in range(DCH):
            w = {"tC": P4ln, "tV": prog.now("tV")} if dc == 0 else None
            @E("tensor", wait=w, inc="tT")
            def _(tn, g16=g16, dc=dc):
                return tn.matmul(p_mm[:, 0:NEXP], xn2[:, dc, g16 * 128:(g16 + 1) * 128],
                                 c_routw[:, dc, :], start=(dc == 0), stop=(dc == DCH - 1))
        @E("vector", wait={"tT": prog.now("tT")}, inc="tV")
        def _(v, g16=g16):
            return v.tensor_copy(lgt[:, g16, :], p_mm[:, 0:NEXP])
    @E("vector", inc="tV")
    def _(v):
        return v.tensor_scalar(lgt[:].rearrange("p a e -> p (a e)"),
                               lgt[:].rearrange("p a e -> p (a e)"), 10000.0, -10000.0,
                               ALU.min, ALU.max)
    if dbg:
        @E("gpsimd", wait={"tV": prog.now("tV")}, inc="dO", inc_by=16)
        def _(g):
            return g.dma_start(out=dbg_lg[:, :], in_=lgt[:].rearrange("p a e -> p (a e)"))
    G = 16
    for g in range(G):
        @E("vector", inc="tV")
        def _(v, g=g):
            return v.tensor_reduce(topv[:, g, 0:1], lgt[:, g, :], AX.X, ALU.max)
    for g in range(G):
        @E("vector", inc="tV")
        def _(v, g=g):
            return v.tensor_scalar(oh1[:, g, :], lgt[:, g, :], topv[:, g, 0:1], None, ALU.is_equal)
    for g in range(G):
        @E("vector", inc="tV")
        def _(v, g=g):
            return v.tensor_mul(fta[:, g, :], oh1[:, g, :], c_i8m[:])
    for g in range(G):
        @E("vector", inc="tV")
        def _(v, g=g):
            return v.tensor_reduce(meta[:, g, 0:1], fta[:, g, :], AX.X, ALU.add)
    for g in range(G):
        @E("vector", inc="tV")
        def _(v, g=g):
            return v.scalar_tensor_tensor(ftb[:, g, :], oh1[:, g, :], -1e30, lgt[:, g, :],
                                          ALU.mult, ALU.add)
    for g in range(G):
        @E("vector", inc="tV")
        def _(v, g=g):
            return v.tensor_reduce(topv[:, g, 1:2], ftb[:, g, :], AX.X, ALU.max)
    for g in range(G):
        @E("vector", inc="tV")
        def _(v, g=g):
            return v.tensor_scalar(oh2[:, g, :], ftb[:, g, :], topv[:, g, 1:2], None, ALU.is_equal)
    for g in range(G):
        @E("vector", inc="tV")
        def _(v, g=g):
            return v.tensor_mul(fta[:, g, :], oh2[:, g, :], c_i8m[:])
    for g in range(G):
        @E("vector", inc="tV")
        def _(v, g=g):
            return v.tensor_reduce(meta[:, g, 1:2], fta[:, g, :], AX.X, ALU.add)
    for g in range(G):
        @E("vector", inc="tV")
        def _(v, g=g):
            return v.tensor_sub(wtmp[:, g, 0:1], topv[:, g, 1:2], topv[:, g, 0:1])
    tV_w = prog.now("tV")
    for g in range(G):
        @E("scalar", wait={"tV": tV_w} if g == 0 else None, inc="tC")
        def _(s, g=g):
            return s.activation(wtmp[:, g, 1:2], wtmp[:, g, 0:1], AF.Exp)
    tC_w = prog.now("tC")
    for g in range(G):
        @E("vector", wait={"tC": tC_w} if g == 0 else None, inc="tV")
        def _(v, g=g):
            return v.tensor_scalar_add(wtmp[:, g, 2:3], wtmp[:, g, 1:2], 1.0 + 1e-12)
    for g in range(G):
        @E("vector", inc="tV")
        def _(v, g=g):
            return v.reciprocal(wtmp[:, g, 3:4], wtmp[:, g, 2:3])
    for g in range(G):
        @E("vector", inc="tV")
        def _(v, g=g):
            return v.tensor_copy(meta[:, g, 2:3], wtmp[:, g, 3:4])
    for g in range(G):
        @E("vector", inc="tV")
        def _(v, g=g):
            return v.tensor_mul(meta[:, g, 3:4], wtmp[:, g, 1:2], wtmp[:, g, 3:4])
    for g in range(G):
        @E("vector", inc="tV")
        def _(v, g=g):
            return v.tensor_add(ohb[:], oh1[:, g, :], oh2[:, g, :]) if False else                    v.tensor_add(fta[:, g, :], oh1[:, g, :], oh2[:, g, :])
    @E("vector", inc="tV")
    def _(v):
        return v.tensor_copy(ftb[:].rearrange("p a e -> p (a e)"),
                             fta[:].rearrange("p a e -> p (a e)"))
    ohball = sbp("ohball", [128, 16, NEXP], BF16)
    @E("vector", inc="tV")
    def _(v):
        return v.tensor_copy(ohball[:].rearrange("p a e -> p (a e)"),
                             fta[:].rearrange("p a e -> p (a e)"))
    tV_ohb = prog.now("tV")
    for g in range(G):
        w = {"tV": tV_ohb} if g == 0 else None
        @E("tensor", wait=w, inc="tT")
        def _(tn, g=g):
            return tn.matmul(p_bc[:, g * NEXP:(g + 1) * NEXP], c_ut[:], ohball[:, g, :],
                             start=True, stop=True)
    for g in range(G):
        @E("tensor", inc="tT")
        def _(tn, g=g):
            return tn.matmul(p_z[:, g * NEXP:(g + 1) * NEXP], onesb[:], ohball[:, g, :],
                             start=True, stop=True)
    @E("vector", wait={"tT": prog.now("tT")}, inc="tV")
    def _(v):
        return v.tensor_copy(preba[:].rearrange("p a e -> p (a e)"), p_bc[:, 0:G * NEXP])
    @E("vector", inc="tV")
    def _(v):
        return v.tensor_copy(trow[:], p_z[:, 0:G * NEXP])
    for g in range(G):
        @E("vector", inc="tV")
        def _(v, g=g):
            return v.tensor_mul(fta[:, g, :], preba[:, g, :], oh1[:, g, :])
    for g in range(G):
        @E("vector", inc="tV")
        def _(v, g=g):
            return v.tensor_reduce(meta[:, g, 4:5], fta[:, g, :], AX.X, ALU.add)
    for g in range(G):
        @E("vector", inc="tV")
        def _(v, g=g):
            return v.tensor_mul(fta[:, g, :], preba[:, g, :], oh2[:, g, :])
    for g in range(G):
        @E("vector", inc="tV")
        def _(v, g=g):
            return v.tensor_reduce(meta[:, g, 5:6], fta[:, g, :], AX.X, ALU.add)
    trowb = sbp("trowb", [1, G * NEXP], BF16)
    tcolb = sbp("tcolb", [128, 1], BF16)
    rowsf = sbp("rowsf", [128, 3], F32)
    @E("vector", inc="tV")
    def _(v):
        return v.tensor_copy(trowb[:], trow[:])
    @E("tensor", wait={"tV": prog.now("tV")}, inc="tT")
    def _(tn):
        return tn.transpose(pt_b[:, 0:1], trowb[:], c_iden[0:1, 0:1])
    @E("vector", wait={"tT": prog.now("tT")}, inc="tV")
    def _(v):
        return v.tensor_copy(tcolb[:], pt_b[:, 0:1])
    p_f = p_bc
    @E("tensor", wait={"tV": prog.now("tV")}, inc="tT")
    def _(tn):
        return tn.matmul(p_f[:, 0:1], c_mcrow[:], tcolb[:], start=True, stop=True)
    @E("tensor", inc="tT")
    def _(tn):
        return tn.matmul(p_f[0:8, 1:2], c_mscsr[:, 0:NEXP], tcolb[:], start=True, stop=True)
    @E("tensor", inc="tT")
    def _(tn):
        return tn.matmul(p_f[0:8, 2:3], c_mscsr[:, NEXP:2 * NEXP], tcolb[:], start=True, stop=True)
    @E("vector", wait={"tT": prog.now("tT")}, inc="tV")
    def _(v):
        return v.tensor_copy(rowsf[:, 0:3], p_f[:, 0:3])
    for k, (dst, width) in enumerate(((crow, G * NEXP), (scro, NEXP), (srow, NEXP))):
        @E("tensor", wait={"tV": prog.now("tV")}, inc="tT")
        def _(tn, k=k):
            return tn.transpose(p_bc2[0:1, 0:128], rowsf[:, k:k + 1], c_idenf[:])
        @E("vector", wait={"tT": prog.now("tT")}, inc="tV")
        def _(v, dst=dst, width=width):
            return v.tensor_copy(dst[:, 0:width], p_bc2[0:1, 0:width])
    @E("tensor", wait={"tV": prog.now("tV")}, inc="tT")
    def _(tn):
        return tn.matmul(p_bc[:, 0:G * NEXP], onesf[:], crow[:, 0:G * NEXP], start=True, stop=True)
    @E("tensor", inc="tT")
    def _(tn):
        return tn.matmul(p_bc2[:, 0:NEXP], onesf[:], srow[:], start=True, stop=True)
    @E("vector", wait={"tT": prog.now("tT")}, inc="tV")
    def _(v):
        return v.tensor_copy(obca[:].rearrange("p a e -> p (a e)"), p_bc[:, 0:G * NEXP])
    @E("vector", inc="tV")
    def _(v):
        return v.tensor_copy(sbc[:], p_bc2[:, 0:NEXP])
    for k, oh in ((0, oh1), (1, oh2)):
        for g in range(G):
            @E("vector", inc="tV")
            def _(v, g=g, oh=oh):
                return v.tensor_mul(fta[:, g, :], obca[:, g, :], oh[:, g, :])
        for g in range(G):
            @E("vector", inc="tV")
            def _(v, g=g, k=k):
                return v.tensor_reduce(slta[:, g, k:k + 1], fta[:, g, :], AX.X, ALU.add)
        for g in range(G):
            @E("vector", inc="tV")
            def _(v, g=g, k=k):
                return v.tensor_add(meta[:, g, 4 + k:5 + k], meta[:, g, 4 + k:5 + k],
                                    slta[:, g, k:k + 1])
    for k, oh in ((0, oh1), (1, oh2)):
        for g in range(G):
            @E("vector", inc="tV")
            def _(v, g=g, oh=oh):
                return v.tensor_mul(fta[:, g, :], sbc[:], oh[:, g, :])
        for g in range(G):
            @E("vector", inc="tV")
            def _(v, g=g, k=k):
                return v.tensor_reduce(slta[:, g, k:k + 1], fta[:, g, :], AX.X, ALU.add)
    tV_sl = prog.now("tV")
    for k in range(2):
        for g in range(G):
            w = {"tV": tV_sl} if (k == 0 and g == 0) else None
            @E("scalar", wait=w, inc="tC")
            def _(s, g=g, k=k):
                return s.add(slta[:, g, k:k + 1], slta[:, g, k:k + 1],
                             meta[:, g, 4 + k:5 + k])
    @E("vector", wait={"tC": prog.now("tC")}, inc="tV")
    def _(v):
        return v.tensor_copy(sltu[:].rearrange("p a k -> p (a k)"),
                             slta[:].rearrange("p a k -> p (a k)"))
    tV_slot = prog.now("tV")
    if dbg:
        dbg_sl = nc.dram_tensor("dbg_sl", [128, 16 * 2], U32, kind="ExternalOutput").ap()
        @E("gpsimd", wait={"tV": tV_slot}, inc="dO", inc_by=16)
        def _(g):
            return g.dma_start(out=dbg_sl[:, :], in_=sltu[:].rearrange("p a k -> p (a k)"))

    for g in range(G):
        for dc in range(DCH):
            @E("tensor", wait={"tV": prog.now("tV")}, inc="tT")
            def _(tn, g=g, dc=dc):
                return tn.transpose(pt_b[:], xn2[:, dc, g * 128:(g + 1) * 128], c_iden[:])
            w = {"tT": prog.now("tT")}
            if dc == 0:
                w["dO"] = prog.now("dO")
            @E("vector", wait=w, inc="tV")
            def _(v, dc=dc):
                return v.tensor_copy(xtm[:, dc * 128:(dc + 1) * 128], pt_b[:])
        @E("gpsimd", wait={"tV": prog.now("tV")}, inc="dO", inc_by=16)
        def _(g_, g=g):
            return g_.indirect_dma_start(
                out=compo[:, :], out_offset=IndirectOffsetOnAxis(ap=sltu[:, g, 0:1], axis=0),
                in_=xtm[:], in_offset=None)
        @E("gpsimd", inc="dO", inc_by=16)
        def _(g_, g=g):
            return g_.indirect_dma_start(
                out=compo[:, :], out_offset=IndirectOffsetOnAxis(ap=sltu[:, g, 1:2], axis=0),
                in_=xtm[:], in_offset=None)
    @E("gpsimd", wait={"tV": prog.now("tV")}, inc="dO", inc_by=16)
    def _(g):
        return g.dma_start(out=rmeta[:, :], in_=meta[:].rearrange("p a k -> p (a k)"))
    @E("gpsimd", inc="dO", inc_by=16)
    def _(g):
        return g.dma_start(out=cnt8[:, :], in_=scro[:])
    @E("gpsimd")
    def _(g):
        g.wait_ge(prog.sem["dO"], prog.now("dO"))
        return None

    esP.close()
    block = es.enter_context(nc.Block())
    prog.build(block)
    es.close()
    return nc


def prep_launch1_inputs(inp):
    x = np.asarray(inp["L0_attn_x"], np.float32)
    Wq = np.asarray(inp["L0_attn_Wq"], np.float32)
    Wk = np.asarray(inp["L0_attn_Wk"], np.float32)
    Wv = np.asarray(inp["L0_attn_Wv"], np.float32)
    Wo = np.asarray(inp["L0_attn_Wo"], np.float32)
    gam = np.asarray(inp["L0_attn_gamma"], np.float32)
    bet = np.asarray(inp["L0_attn_beta"], np.float32)
    g2 = np.asarray(inp["L0_ffn_ln_gamma"], np.float32)
    b2 = np.asarray(inp["L0_ffn_ln_beta"], np.float32)
    routw = np.asarray(inp["L0_ffn_router_w"], np.float32)

    wqkv = np.stack([w.transpose(1, 0, 2).reshape(D, D) for w in (Wq, Wk, Wv)])
    import ml_dtypes
    iden = np.eye(128, dtype=ml_dtypes.bfloat16)
    ut = np.triu(np.ones((128, 128)), 1).astype(ml_dtypes.bfloat16)
    iota = np.arange(128, dtype=np.float32).reshape(128, 1)
    i8m = np.tile(np.arange(8, dtype=np.float32), (128, 1))
    gg = np.arange(128) // NEXP
    ee = np.arange(128) % NEXP
    mcrow = ((gg[:, None] > gg[None, :]) & (ee[:, None] == ee[None, :])).astype(ml_dtypes.bfloat16).T
    msc = (ee[:, None] == np.arange(NEXP)[None, :]).astype(ml_dtypes.bfloat16)
    msr = (ee[:, None] < np.arange(NEXP)[None, :]).astype(ml_dtypes.bfloat16)
    mscsr = np.concatenate([msc, msr], 1)
    idenf = np.eye(128, dtype=np.float32)
    pp = lambda v: np.ascontiguousarray(v.reshape(DCH, 128).T)

    maps = []
    for c in range(NCORES):
        b, half = c // 2, c % 2
        qoff = half * TPC
        thr = (qoff + np.arange(TPC, dtype=np.float32) + 1.0).reshape(1, TPC)
        ckj = np.zeros((NQT, SEQ), np.float32)
        for j in range(NQT):
            ckj[j, qoff + (j + 1) * QT:] = -100.0
        ckey = np.ascontiguousarray(
            ckj.reshape(NQT, SEQ // 128, 128).transpose(2, 0, 1).reshape(128, NQT * (SEQ // 128)))
        maps.append({
            "xT": np.ascontiguousarray(x[b].T),
            "wqkv": wqkv,
            "wo": Wo,
            "gam": pp(gam), "bet": pp(bet), "g2": pp(g2), "b2": pp(b2),
            "routw": np.ascontiguousarray(routw.T),
            "thr": thr, "ckey": ckey,
            "iden": iden, "ut128": ut, "iota": iota, "i8m": i8m,
            "mcrow": mcrow, "mscsr": mscsr, "idenf": idenf,
            "qpar": np.array([[half]], np.uint32),
        })
    return maps


TT2 = 1024
NTT2 = CAP // TT2
NW2 = CAP // 128


def build_launch2(dbg=False, maxtt=NTT2, parts=4):
    nc = bass.Bass("TRN2", target_bir_lowering=False, debug=False)
    compall = nc.dram_tensor("compall", [NCORES * 2 * TPC, D], BF16, kind="ExternalInput").ap()
    cntsT = nc.dram_tensor("cntsT", [NEXP, NCORES], F32, kind="ExternalInput").ap()
    sel = nc.dram_tensor("sel", [NEXP, 2], F32, kind="ExternalInput").ap()
    ut8f = nc.dram_tensor("ut8f", [NCORES, NCORES], F32, kind="ExternalInput").ap()
    iota40 = nc.dram_tensor("iota40", [128, NW2], F32, kind="ExternalInput").ap()
    idenf = nc.dram_tensor("idenf", [128, 128], F32, kind="ExternalInput").ap()
    iden = nc.dram_tensor("iden", [128, 128], BF16, kind="ExternalInput").ap()
    w12c = nc.dram_tensor("w12c", [64, D, 128], F32, kind="ExternalInput").ap()
    w3c = nc.dram_tensor("w3c", [8, 32, 128, 128], F32, kind="ExternalInput").ap()
    ybo = nc.dram_tensor("ybo", [CAP, D], BF16, kind="ExternalOutput").ap()

    es = ExitStack()
    sb = lambda name, shape, dt: es.enter_context(nc.sbuf_tensor(name, shape, dt))
    psu = lambda name, shape, dt=F32: es.enter_context(nc.psum_tensor(name, shape, dt))

    c_iden = sb("c_iden", [128, 128], BF16)
    c_idenf = sb("c_idenf", [128, 128], F32)
    c_iota = sb("c_iota", [128, NW2], F32)
    c_cntsT = sb("c_cntsT", [NEXP, NCORES], F32)
    c_sel = sb("c_sel", [NEXP, 2], F32)
    c_ut8 = sb("c_ut8", [NCORES, NCORES], F32)
    onesf = sb("onesf", [1, 128], F32)
    rows = sb("rows", [1, 3 * NCORES], F32)
    colt = sb("colt", [NCORES, 2], F32)
    bca = sb("bca", [128, 3, NCORES], F32)
    gla = sb("gla", [128, NCORES, NW2], F32)
    glm = sb("glm", [128, NCORES, NW2], F32)
    glg = sb("glg", [128, NCORES, NW2], F32)
    gsc = sb("gsc", [128, NW2], F32)
    idxu = sb("idxu", [128, NW2], U32)
    w3sb = sb("w3sb", [128, 8, 32, 128], BF16)
    hT = sb("hT", [128, 32, TT2], BF16)
    w12sb = sb("w12sb", [128, 3, 2, DCH, 128], BF16)
    xg = sb("xg", [128, 2, D], BF16)
    xgT = sb("xgT", [128, DCH, TT2], BF16)
    gsl = sb("gsl", [128, 512], F32)
    ybt = sb("ybt", [128, DCH, TT2], BF16)
    ytm = sb("ytm", [128, 2, D], BF16)

    p_a = psu("p_a", [128, 512])
    p_b = psu("p_b", [128, 512])
    p_c = psu("p_c", [128, 512])
    pt_b = es.enter_context(nc.psum_tensor("pt_b", [128, 128], BF16))
    pt_f = es.enter_context(nc.psum_tensor("pt_f", [128, 128], F32))

    prog = Prog(nc, es)
    E = prog.emit

    for dst, srcv in [(c_iden, iden[:, :]), (c_idenf, idenf[:, :]), (c_iota, iota40[:, :]),
                      (c_cntsT, cntsT[:, :]), (c_sel, sel[:, :]), (c_ut8, ut8f[:, :])]:
        @E("sync", inc="dS", inc_by=16)
        def _(s, dst=dst, srcv=srcv):
            return s.dma_start(out=dst[:], in_=srcv)

    @E("vector", inc="tV")
    def _(v):
        return v.memset(onesf[:], 1.0)
    CONSTS = prog.now("dS", "tV")

    for dcw in range(DCH):
        @E("gpsimd", inc="dG", inc_by=16)
        def _(g, dcw=dcw):
            return g.dma_start(out=w3sb[:, dcw, :, :],
                               in_=w3c.rearrange("dc fc p pd -> dc p fc pd")[dcw, :, :, :])
    dG_w3 = prog.now("dG")

    @E("tensor", wait=dict(CONSTS), inc="tT")
    def _(tn):
        return tn.matmul(pt_f[0:1, 0:NCORES], c_sel[:, 0:1], c_cntsT[:, :], start=True, stop=True)
    @E("tensor", inc="tT")
    def _(tn):
        return tn.matmul(pt_f[0:1, NCORES:2 * NCORES], c_sel[:, 1:2], c_cntsT[:, :],
                         start=True, stop=True)
    @E("vector", wait={"tT": prog.now("tT")}, inc="tV")
    def _(v):
        return v.tensor_copy(rows[:, 0:2 * NCORES], pt_f[0:1, 0:2 * NCORES])
    @E("tensor", wait={"tV": prog.now("tV")}, inc="tT")
    def _(tn):
        return tn.transpose(pt_f[0:NCORES, 0:1], rows[0:1, 0:NCORES], c_idenf[0:1, 0:1])
    @E("vector", wait={"tT": prog.now("tT")}, inc="tV")
    def _(v):
        return v.tensor_copy(colt[:, 0:1], pt_f[0:NCORES, 0:1])
    @E("tensor", wait={"tV": prog.now("tV")}, inc="tT")
    def _(tn):
        return tn.matmul(pt_f[0:NCORES, 1:2], c_ut8[:, :], colt[:, 0:1], start=True, stop=True)
    @E("vector", wait={"tT": prog.now("tT")}, inc="tV")
    def _(v):
        return v.tensor_copy(colt[:, 1:2], pt_f[0:NCORES, 1:2])
    @E("tensor", wait={"tV": prog.now("tV")}, inc="tT")
    def _(tn):
        return tn.transpose(pt_f[0:1, 0:NCORES], colt[:, 1:2], c_idenf[0:NCORES, 0:NCORES])
    @E("vector", wait={"tT": prog.now("tT")}, inc="tV")
    def _(v):
        return v.tensor_copy(rows[:, 2 * NCORES:3 * NCORES], pt_f[0:1, 0:NCORES])
    @E("tensor", wait={"tV": prog.now("tV")}, inc="tT")
    def _(tn):
        return tn.matmul(p_a[:, 0:3 * NCORES], onesf[:], rows[:, 0:3 * NCORES],
                         start=True, stop=True)
    @E("vector", wait={"tT": prog.now("tT")}, inc="tV")
    def _(v):
        return v.tensor_copy(bca[:].rearrange("p a c -> p (a c)"), p_a[:, 0:3 * NCORES])

    def bounce():
        tv = prog.now("tV")
        @E("scalar", wait={"tV": tv}, inc="tC")
        def _(s):
            return s.add(gsc[0:1, 0:1], gsc[0:1, 0:1], 0.0)
        tc = prog.now("tC")
        return {"tC": tc}
    BW = bounce()
    for c in range(NCORES):
        @E("vector", wait=(BW if c == 0 else None), inc="tV")
        def _(v, c=c):
            return v.tensor_scalar(gla[:, c, :], c_iota[:], bca[:, 2, c:c + 1], None, ALU.subtract)
    BW = bounce()
    for c in range(NCORES):
        @E("vector", wait=(BW if c == 0 else None), inc="tV")
        def _(v, c=c):
            return v.tensor_scalar(glm[:, c, :], gla[:, c, :], bca[:, 0, c:c + 1], None, ALU.is_lt)
    BW = bounce()
    for c in range(NCORES):
        @E("vector", wait=(BW if c == 0 else None), inc="tV")
        def _(v, c=c):
            return v.tensor_scalar(glg[:, c, :], gla[:, c, :], 0.0, None, ALU.is_ge)
    BW = bounce()
    for c in range(NCORES):
        @E("vector", wait=(BW if c == 0 else None), inc="tV")
        def _(v, c=c):
            return v.tensor_mul(glm[:, c, :], glm[:, c, :], glg[:, c, :])
    BW = bounce()
    for c in range(NCORES):
        @E("vector", wait=(BW if c == 0 else None), inc="tV")
        def _(v, c=c):
            return v.tensor_scalar(gla[:, c, :], gla[:, c, :], bca[:, 1, c:c + 1], None, ALU.add)
    BW = bounce()
    for c in range(NCORES):
        @E("vector", wait=(BW if c == 0 else None), inc="tV")
        def _(v, c=c):
            return v.tensor_scalar_add(gla[:, c, :], gla[:, c, :], float(c * 2 * TPC))
    BW = bounce()
    for c in range(NCORES):
        @E("vector", wait=(BW if c == 0 else None), inc="tV")
        def _(v, c=c):
            return v.tensor_mul(gla[:, c, :], gla[:, c, :], glm[:, c, :])
    BW = bounce()
    for c in range(4):
        @E("vector", wait=(BW if c == 0 else None), inc="tV")
        def _(v, c=c):
            return v.tensor_add(glg[:, c, :], gla[:, c, :], gla[:, c + 4, :])
    BW = bounce()
    for c in range(2):
        @E("vector", wait=(BW if c == 0 else None), inc="tV")
        def _(v, c=c):
            return v.tensor_add(gla[:, c, :], glg[:, c, :], glg[:, c + 2, :])
    BW = bounce()
    @E("vector", wait=BW, inc="tV")
    def _(v):
        return v.tensor_add(gla[:, 0, :], gla[:, 0, :], gla[:, 1, :])
    @E("scalar", wait={"tV": prog.now("tV")}, inc="tC")
    def _(s):
        return s.copy(gsc[:], gla[:, 0, :])
    @E("vector", wait={"tC": prog.now("tC")}, inc="tV")
    def _(v):
        return v.tensor_copy(idxu[:], gsc[:])
    IDXV = prog.now("tV")
    if dbg:
        dbg_rows = nc.dram_tensor("dbg_rows", [1, 3 * NCORES], F32, kind="ExternalOutput").ap()
        dbg_idx = nc.dram_tensor("dbg_idx", [128, NW2], U32, kind="ExternalOutput").ap()
        @E("gpsimd", wait={"tV": IDXV}, inc="dO", inc_by=16)
        def _(g):
            return g.dma_start(out=dbg_rows[:, :], in_=rows[:, 0:3 * NCORES])
        @E("gpsimd", inc="dO", inc_by=16)
        def _(g):
            return g.dma_start(out=dbg_idx[:, :], in_=idxu[:])

    for tt in range(maxtt):
        for q in range(TT2 // 128):
            gb = q % 2
            col = tt * (TT2 // 128) + q
            @E("gpsimd", wait={"tV": IDXV, "tT": prog.now("tT")}, inc="dG", inc_by=16)
            def _(g, gb=gb, col=col):
                return g.indirect_dma_start(
                    out=xg[:, gb, :], out_offset=None,
                    in_=compall[:, :],
                    in_offset=IndirectOffsetOnAxis(ap=idxu[:, col:col + 1], axis=0))
            if parts >= 1.5:
                for dc in range(DCH):
                    w = {"tV": prog.now("tV")}
                    if dc == 0:
                        w["dG"] = prog.now("dG")
                    @E("tensor", wait=w, inc="tT")
                    def _(tn, gb=gb, dc=dc):
                        return tn.transpose(pt_b[:], xg[:, gb, dc * 128:(dc + 1) * 128], c_iden[:])
                    @E("vector", wait={"tT": prog.now("tT")}, inc="tV")
                    def _(v, dc=dc, q=q):
                        return v.tensor_copy(xgT[:, dc, q * 128:(q + 1) * 128], pt_b[:])
        tV_xg = prog.now("tV")
        if parts < 2:
            continue

        for fc in range(32):
            wb = fc % 3
            @E("gpsimd", wait={"tT": prog.now("tT")}, inc="dG", inc_by=16)
            def _(g, fc=fc, wb=wb):
                return g.dma_start(out=w12sb[:, wb, 0, :, :],
                                   in_=w12c.rearrange("f (c p) w -> f p c w", p=128)[fc, :, :, :])
            @E("gpsimd", inc="dG", inc_by=16)
            def _(g, fc=fc, wb=wb):
                return g.dma_start(out=w12sb[:, wb, 1, :, :],
                                   in_=w12c.rearrange("f (c p) w -> f p c w", p=128)[fc + 32, :, :, :])
            dG_w = prog.now("dG")
            for sub in range(TT2 // 512):
                for dc in range(DCH):
                    w = {"dG": dG_w, "tV": prog.now("tV"), "tC": prog.now("tC")} if dc == 0 else None
                    @E("tensor", wait=w, inc="tT")
                    def _(tn, fc=fc, wb=wb, dc=dc, sub=sub):
                        return tn.matmul(p_a[:], w12sb[:, wb, 0, dc, :],
                                         xgT[:, dc, sub * 512:(sub + 1) * 512],
                                         start=(dc == 0), stop=(dc == DCH - 1))
                for dc in range(DCH):
                    @E("tensor", inc="tT")
                    def _(tn, fc=fc, wb=wb, dc=dc, sub=sub):
                        return tn.matmul(p_b[:], w12sb[:, wb, 1, dc, :],
                                         xgT[:, dc, sub * 512:(sub + 1) * 512],
                                         start=(dc == 0), stop=(dc == DCH - 1))
                @E("scalar", wait={"tT": prog.now("tT"), "tV": prog.now("tV")}, inc="tC")
                def _(s, sub=sub):
                    return s.activation(gsl[:], p_a[:], AF.Silu)
                @E("vector", wait={"tC": prog.now("tC")}, inc="tV")
                def _(v, fc=fc, sub=sub):
                    return v.tensor_mul(hT[:, fc, sub * 512:(sub + 1) * 512], p_b[:], gsl[:])
        tV_h = prog.now("tV")
        if parts < 3:
            continue

        for dc in range(DCH):
            for sub in range(TT2 // 512):
                for fc in range(32):
                    w = {"tV": prog.now("tV")} if fc == 0 else None
                    @E("tensor", wait=w, inc="tT")
                    def _(tn, dc=dc, sub=sub, fc=fc):
                        return tn.matmul(p_c[:], w3sb[:, dc, fc, :],
                                         hT[:, fc, sub * 512:(sub + 1) * 512],
                                         start=(fc == 0), stop=(fc == 31))
                @E("vector", wait={"tT": prog.now("tT")}, inc="tV")
                def _(v, dc=dc, sub=sub):
                    return v.tensor_copy(ybt[:, dc, sub * 512:(sub + 1) * 512], p_c[:])
        if parts < 4:
            continue
        for q in range(TT2 // 128):
            ob = q % 2
            for dc in range(DCH):
                w = {"tV": prog.now("tV")}
                if dc == 0:
                    w["dO"] = prog.now("dO")
                @E("tensor", wait=w, inc="tT")
                def _(tn, q=q, dc=dc):
                    return tn.transpose(pt_b[:], ybt[:, dc, q * 128:(q + 1) * 128], c_iden[:])
                @E("vector", wait={"tT": prog.now("tT")}, inc="tV")
                def _(v, dc=dc, ob=ob):
                    return v.tensor_copy(ytm[:, ob, dc * 128:(dc + 1) * 128], pt_b[:])
            @E("gpsimd", wait={"tV": prog.now("tV")}, inc="dO", inc_by=16)
            def _(g, tt=tt, q=q, ob=ob):
                return g.dma_start(out=ybo[tt * TT2 + q * 128:tt * TT2 + (q + 1) * 128, :],
                                   in_=ytm[:, ob, :])

    @E("gpsimd")
    def _(g):
        g.wait_ge(prog.sem["dO"], prog.now("dO"))
        return None

    block = es.enter_context(nc.Block())
    prog.build(block)
    es.close()
    return nc


def prep_launch2_inputs(l1_results, inp):
    import ml_dtypes
    w12 = np.asarray(inp["L0_ffn_w12"], np.float32)
    w3 = np.asarray(inp["L0_ffn_w3"], np.float32)
    compall = np.concatenate([np.asarray(r["compo"]) for r in l1_results], 0)
    cnts = np.stack([np.asarray(r["cnt8"][0]) for r in l1_results])
    cntsT = np.ascontiguousarray(cnts.T)
    ut8 = np.triu(np.ones((NCORES, NCORES), np.float32), 1)
    iota40 = np.ascontiguousarray(
        np.arange(CAP, dtype=np.float32).reshape(NW2, 128).T)
    idenf = np.eye(128, dtype=np.float32)
    iden = np.eye(128, dtype=ml_dtypes.bfloat16)
    maps = []
    for e in range(NCORES):
        selv = np.zeros((NEXP, 2), np.float32)
        selv[e, 0] = 1.0
        selv[:e, 1] = 1.0
        w12T = np.ascontiguousarray(w12[e].T)
        w12ch = np.ascontiguousarray(
            w12T.reshape(D, 64, 128).transpose(1, 0, 2))
        w3T = np.ascontiguousarray(w3[e].T)
        w3ch = np.ascontiguousarray(
            w3T.reshape(32, 128, 8, 128).transpose(2, 0, 1, 3))
        maps.append({
            "compall": compall, "cntsT": cntsT, "sel": selv, "ut8f": ut8,
            "iota40": iota40, "idenf": idenf, "iden": iden,
            "w12c": w12ch, "w3c": w3ch,
        })
    return maps

def build_launch3():
    nc = bass.Bass("TRN2", target_bir_lowering=False, debug=False)
    x1i = nc.dram_tensor("x1i", [D, TPC], F32, kind="ExternalInput").ap()
    yball = nc.dram_tensor("yball", [NEXP * CAP, D], BF16, kind="ExternalInput").ap()
    rmeta = nc.dram_tensor("rmeta", [128, 16 * 6], F32, kind="ExternalInput").ap()
    cnts8 = nc.dram_tensor("cnts8", [NCORES, NEXP], F32, kind="ExternalInput").ap()
    maskc = nc.dram_tensor("maskc", [NCORES, 1], F32, kind="ExternalInput").ap()
    i8m = nc.dram_tensor("i8m", [128, NEXP], F32, kind="ExternalInput").ap()
    iden = nc.dram_tensor("iden", [128, 128], BF16, kind="ExternalInput").ap()
    x2o = nc.dram_tensor("x2o", [D, TPC], F32, kind="ExternalOutput").ap()

    es = ExitStack()
    sb = lambda name, shape, dt: es.enter_context(nc.sbuf_tensor(name, shape, dt))
    psu = lambda name, shape, dt=F32: es.enter_context(nc.psum_tensor(name, shape, dt))

    c_iden = sb("c_iden", [128, 128], BF16)
    c_i8m = sb("c_i8m", [128, NEXP], F32)
    c_cnts = sb("c_cnts", [NCORES, NEXP], F32)
    c_mask = sb("c_mask", [NCORES, 1], F32)
    onesf = sb("onesf", [1, 128], F32)
    x1sb = sb("x1sb", [128, DCH, TPC], F32)
    meta = sb("meta", [128, 16, 6], F32)
    orow = sb("orow", [1, NEXP], F32)
    obc = sb("obc", [128, NEXP], F32)
    oha = sb("oha", [128, 16, NEXP], F32)
    ohb2 = sb("ohb2", [128, 16, NEXP], F32)
    gpa = sb("gpa", [128, 16, 2], F32)
    kpa = sb("kpa", [128, 16, 2], F32)
    wka = sb("wka", [128, 16, 2], F32)
    idxf = sb("idxf", [128, 16, 2], F32)
    idxu = sb("idxu", [128, 16, 2], U32)
    gt1 = sb("gt1", [128, 2, D], BF16)
    gt2 = sb("gt2", [128, 2, D], BF16)
    ctm = sb("ctm", [128, D], BF16)

    p_a = psu("p_a", [128, 512])
    pt_b = es.enter_context(nc.psum_tensor("pt_b", [128, 128], BF16))
    pt_f = es.enter_context(nc.psum_tensor("pt_f", [128, 128], F32))

    prog = Prog(nc, es)
    E = prog.emit

    for dst, srcv in [(c_iden, iden[:, :]), (c_i8m, i8m[:, :]), (c_cnts, cnts8[:, :]),
                      (c_mask, maskc[:, :])]:
        @E("sync", inc="dS", inc_by=16)
        def _(s, dst=dst, srcv=srcv):
            return s.dma_start(out=dst[:], in_=srcv)

    @E("sync", inc="dS", inc_by=16)
    def _(s):
        return s.dma_start(out=meta[:, :, :].rearrange("p a k -> p (a k)"), in_=rmeta[:, :])

    @E("sync", inc="dS", inc_by=16)
    def _(s):
        return s.dma_start(out=x1sb[:, :, :],
                           in_=x1i.rearrange("(c p) t -> p c t", p=128))

    @E("vector", inc="tV")
    def _(v):
        return v.memset(onesf[:], 1.0)
    CONSTS = prog.now("dS", "tV")

    @E("tensor", wait=dict(CONSTS), inc="tT")
    def _(tn):
        return tn.matmul(pt_f[0:1, 0:NEXP], c_mask[:, :], c_cnts[:, :], start=True, stop=True)
    @E("vector", wait={"tT": prog.now("tT")}, inc="tV")
    def _(v):
        return v.tensor_copy(orow[:], pt_f[0:1, 0:NEXP])
    @E("tensor", wait={"tV": prog.now("tV")}, inc="tT")
    def _(tn):
        return tn.matmul(pt_f[:, 0:NEXP], onesf[:], orow[:], start=True, stop=True)
    @E("vector", wait={"tT": prog.now("tT")}, inc="tV")
    def _(v):
        return v.tensor_copy(obc[:], pt_f[:, 0:NEXP])

    G = 16
    for k, oh in ((0, oha), (1, ohb2)):
        for g in range(G):
            @E("vector", inc="tV")
            def _(v, g=g, k=k, oh=oh):
                return v.tensor_scalar(oh[:, g, :], c_i8m[:], meta[:, g, k:k + 1],
                                       None, ALU.is_equal)
    for k, oh in ((0, oha), (1, ohb2)):
        for g in range(G):
            @E("vector", inc="tV")
            def _(v, g=g, oh=oh):
                return v.tensor_mul(oh[:, g, :], oh[:, g, :], obc[:])
        for g in range(G):
            @E("vector", inc="tV")
            def _(v, g=g, k=k, oh=oh):
                return v.tensor_reduce(gpa[:, g, k:k + 1], oh[:, g, :], AX.X, ALU.add)
    for k, oh in ((0, oha), (1, ohb2)):
        for g in range(G):
            @E("vector", inc="tV")
            def _(v, g=g, k=k, oh=oh):
                return v.tensor_scalar(oh[:, g, :], c_i8m[:], meta[:, g, k:k + 1],
                                       None, ALU.is_equal)
    for k in range(2):
        for g in range(G):
            @E("vector", inc="tV")
            def _(v, g=g, k=k):
                return v.tensor_add(gpa[:, g, k:k + 1], gpa[:, g, k:k + 1],
                                    meta[:, g, 4 + k:5 + k])
    for k in range(2):
        for g in range(G):
            @E("vector", inc="tV")
            def _(v, g=g, k=k):
                return v.tensor_scalar(kpa[:, g, k:k + 1], gpa[:, g, k:k + 1],
                                       float(CAP), None, ALU.is_lt)
    for k in range(2):
        for g in range(G):
            @E("vector", inc="tV")
            def _(v, g=g, k=k):
                return v.tensor_scalar(idxf[:, g, k:k + 1], meta[:, g, k:k + 1],
                                       float(CAP), None, ALU.mult)
    for k in range(2):
        for g in range(G):
            @E("vector", inc="tV")
            def _(v, g=g, k=k):
                return v.tensor_add(idxf[:, g, k:k + 1], idxf[:, g, k:k + 1],
                                    gpa[:, g, k:k + 1])
    for k in range(2):
        for g in range(G):
            @E("vector", inc="tV")
            def _(v, g=g, k=k):
                return v.tensor_mul(idxf[:, g, k:k + 1], idxf[:, g, k:k + 1],
                                    kpa[:, g, k:k + 1])
    for k in range(2):
        for g in range(G):
            @E("vector", inc="tV")
            def _(v, g=g, k=k):
                return v.tensor_mul(wka[:, g, k:k + 1], meta[:, g, 2 + k:3 + k],
                                    kpa[:, g, k:k + 1])
    tV_idx = prog.now("tV")
    @E("scalar", wait={"tV": tV_idx}, inc="tC")
    def _(s):
        return s.copy(gpa[:].rearrange("p a k -> p (a k)"),
                      idxf[:].rearrange("p a k -> p (a k)"))
    @E("vector", wait={"tC": prog.now("tC")}, inc="tV")
    def _(v):
        return v.tensor_copy(idxu[:].rearrange("p a k -> p (a k)"),
                             gpa[:].rearrange("p a k -> p (a k)"))
    IDXV = prog.now("tV")

    for g in range(G):
        gb = g % 2
        @E("gpsimd", wait={"tV": IDXV, "tT": prog.now("tT")}, inc="dG", inc_by=16)
        def _(g_, g=g, gb=gb):
            return g_.indirect_dma_start(
                out=gt1[:, gb, :], out_offset=None, in_=yball[:, :],
                in_offset=IndirectOffsetOnAxis(ap=idxu[:, g, 0:1], axis=0))
        @E("gpsimd", inc="dG", inc_by=16)
        def _(g_, g=g, gb=gb):
            return g_.indirect_dma_start(
                out=gt2[:, gb, :], out_offset=None, in_=yball[:, :],
                in_offset=IndirectOffsetOnAxis(ap=idxu[:, g, 1:2], axis=0))
        dG_g = prog.now("dG")
        @E("vector", wait={"dG": dG_g, "tT": prog.now("tT")}, inc="tV")
        def _(v, g=g, gb=gb):
            return v.tensor_scalar(ctm[:], gt1[:, gb, :], wka[:, g, 0:1], None, ALU.mult)
        @E("vector", inc="tV")
        def _(v, g=g, gb=gb):
            return v.scalar_tensor_tensor(ctm[:], gt2[:, gb, :], wka[:, g, 1:2], ctm[:],
                                          ALU.mult, ALU.add)
        tV_ctm = prog.now("tV")
        for dc in range(DCH):
            @E("tensor", wait={"tV": tV_ctm} if dc == 0 else {"tV": prog.now("tV")}, inc="tT")
            def _(tn, dc=dc):
                return tn.transpose(pt_b[:], ctm[:, dc * 128:(dc + 1) * 128], c_iden[:])
            @E("vector", wait={"tT": prog.now("tT")}, inc="tV")
            def _(v, g=g, dc=dc):
                return v.tensor_add(x1sb[:, dc, g * 128:(g + 1) * 128],
                                    x1sb[:, dc, g * 128:(g + 1) * 128], pt_b[:])
    @E("gpsimd", wait={"tV": prog.now("tV")}, inc="dO", inc_by=16)
    def _(g):
        return g.dma_start(out=x2o.rearrange("(c p) t -> p c t", p=128), in_=x1sb[:, :, :])
    @E("gpsimd")
    def _(g):
        g.wait_ge(prog.sem["dO"], prog.now("dO"))
        return None

    block = es.enter_context(nc.Block())
    prog.build(block)
    es.close()
    return nc


def prep_launch3_inputs(l1_results, l2_results):
    import ml_dtypes
    yball = np.concatenate([np.asarray(r["ybo"]) for r in l2_results], 0)
    cnts = np.stack([np.asarray(r["cnt8"][0]) for r in l1_results])
    iden = np.eye(128, dtype=ml_dtypes.bfloat16)
    i8m = np.tile(np.arange(8, dtype=np.float32), (128, 1))
    maps = []
    for c in range(NCORES):
        mk = np.zeros((NCORES, 1), np.float32)
        mk[:c, 0] = 1.0
        maps.append({
            "x1i": np.asarray(l1_results[c]["x1o"]),
            "yball": yball,
            "rmeta": np.asarray(l1_results[c]["rmeta"]),
            "cnts8": cnts, "maskc": mk, "i8m": i8m, "iden": iden,
        })
    return maps

def build_launch4():
    nc = bass.Bass("TRN2", target_bir_lowering=False, debug=False)
    x2b = nc.dram_tensor("x2b", [D, SEQ], F32, kind="ExternalInput").ap()
    wqkv = nc.dram_tensor("wqkv", [3, D, D], F32, kind="ExternalInput").ap()
    wo = nc.dram_tensor("wo", [D, D], F32, kind="ExternalInput").ap()
    bqkv = nc.dram_tensor("bqkv", [128, 3, NH // 2], F32, kind="ExternalInput").ap()
    bo = nc.dram_tensor("bo", [128, DCH], F32, kind="ExternalInput").ap()
    iden = nc.dram_tensor("iden", [128, 128], BF16, kind="ExternalInput").ap()
    x3o = nc.dram_tensor("x3o", [D, TPC], F32, kind="ExternalOutput").ap()
    xnd = nc.dram_tensor("xnd", [D, SEQ], BF16, kind="Internal").ap()

    es = ExitStack()
    sb = lambda name, shape, dt: es.enter_context(nc.sbuf_tensor(name, shape, dt))
    psu = lambda name, shape, dt=F32: es.enter_context(nc.psum_tensor(name, shape, dt))

    xn = sb("xn", [128, DCH, SEQ], BF16)
    attn = sb("attn", [128, DCH, TPC], BF16)
    e1b = sb("e1b", [128, 32, QT], BF16)
    m2b = e1b
    c_iden = sb("c_iden", [128, 128], BF16)
    c_bqkv = sb("c_bqkv", [128, 3, NH // 2], F32)
    c_bo = sb("c_bo", [128, DCH], F32)
    onesb = sb("onesb", [128, 1], BF16)
    onesf = sb("onesf", [1, 128], F32)
    murow = sb("murow", [1, QT], F32)
    rsrow = sb("rsrow", [1, QT], F32)
    tmprow = sb("tmprow", [1, QT], F32)
    mub = sb("mub", [128, QT], F32)
    rstdb = sb("rstdb", [128, QT], F32)
    xnq = sb("xnq", [128, 2, DCH, QT], BF16)
    ptile = sb("ptile", [128, QT], F32)
    rzb = sb("rzb", [128, QT], F32)
    zrow = sb("zrow", [1, QT], F32)
    vsum = sb("vsum", [128, 1], F32)
    esP = ExitStack()
    sbp = lambda name, shape, dt: esP.enter_context(nc.sbuf_tensor(name, shape, dt))
    xf = sbp("xf", [128, 1, DCH, QT], F32)
    xsq = sbp("xsq", [128, DCH, QT], BF16)
    xb16 = sbp("xb16", [128, DCH, QT], BF16)

    p_mm = psu("p_mm", [128, QT])
    p_mm2 = psu("p_mm2", [128, QT])
    p_s1 = psu("p_s1", [1, QT])
    p_s2 = psu("p_s2", [1, QT])
    p_bc = psu("p_bc", [128, QT])
    p_bc2 = psu("p_bc2", [128, QT])
    p_pv = psu("p_pv", [128, QT])
    pt_b4 = es.enter_context(nc.psum_tensor("pt_b4", [128, 128], BF16))
    p_z = p_s1

    prog = Prog(nc, es)
    E = prog.emit

    for dst, srcv in [(c_iden, iden[:, :]), (c_bo, bo[:, :])]:
        @E("sync", inc="dS", inc_by=16)
        def _(s, dst=dst, srcv=srcv):
            return s.dma_start(out=dst[:], in_=srcv)
    @E("sync", inc="dS", inc_by=16)
    def _(s):
        return s.dma_start(out=c_bqkv[:, :, :], in_=bqkv[:, :, :])
    @E("vector", inc="tV")
    def _(v):
        return v.memset(onesb[:], 1.0)
    @E("vector", inc="tV")
    def _(v):
        return v.memset(onesf[:], 1.0)
    CONSTS = prog.now("dS", "tV")

    NT = SEQ // QT
    tv_tile_done = [0] * NT
    for t in range(NT):
        buf = 0
        w = {"tV": tv_tile_done[t - 1]} if t >= 1 else {}
        @E("sync", wait=w, inc="dS", inc_by=16)
        def _(s, t=t, buf=buf):
            return s.dma_start(out=xf[:, buf, :, :],
                               in_=x2b.rearrange("(c p) (n q) -> p n c q", p=128, q=QT)[:, t, :, :])
        @E("vector", wait={**({} if t else CONSTS), "dS": prog.now("dS"),
                           "tC": prog.now("tC"), "tT": prog.now("tT")}, inc="tV")
        def _(v, buf=buf):
            return v.tensor_copy(xb16[:].rearrange("p c q -> p (c q)"),
                                 xf[:, buf].rearrange("p c q -> p (c q)"))
        @E("scalar", wait={"tV": prog.now("tV")}, inc="tC")
        def _(s):
            return s.activation(xsq[:].rearrange("p c q -> p (c q)"),
                                xb16[:].rearrange("p c q -> p (c q)"), AF.Square)
        w_mm = {"tV": prog.now("tV"), "tC": prog.now("tC")}
        for dc in range(DCH):
            @E("tensor", wait=(w_mm if dc == 0 else None), inc="tT")
            def _(tn, dc=dc):
                return tn.matmul(p_s1[:], onesb[:], xb16[:, dc, :],
                                 start=(dc == 0), stop=(dc == DCH - 1))
        for dc in range(DCH):
            @E("tensor", inc="tT")
            def _(tn, dc=dc):
                return tn.matmul(p_s2[:], onesb[:], xsq[:, dc, :],
                                 start=(dc == 0), stop=(dc == DCH - 1))
        emit_ln_rows(prog, p_s1, p_s2, murow, rsrow, tmprow, D, 1e-5)
        @E("tensor", wait={"tV": prog.now("tV")}, inc="tT")
        def _(tn):
            return tn.matmul(p_bc[:], onesf[:], murow[:], start=True, stop=True)
        @E("tensor", inc="tT")
        def _(tn):
            return tn.matmul(p_bc2[:], onesf[:], rsrow[:], start=True, stop=True)
        @E("vector", wait={"tT": prog.now("tT")}, inc="tV")
        def _(v):
            return v.tensor_copy(mub[:], p_bc[:])
        @E("vector", inc="tV")
        def _(v):
            return v.tensor_copy(rstdb[:], p_bc2[:])
        for dc in range(DCH):
            @E("vector", wait={"tV": prog.now("tV")} if dc == 0 else None, inc="tV")
            def _(v, dc=dc, buf=buf):
                return v.tensor_sub(xf[:, buf, dc, :], xf[:, buf, dc, :], mub[:])
        for dc in range(DCH):
            @E("vector", inc="tV")
            def _(v, dc=dc, buf=buf, t=t):
                return v.tensor_mul(xn[:, dc, t * QT:(t + 1) * QT], xf[:, buf, dc, :], rstdb[:])
        @E("gpsimd", wait={"tV": prog.now("tV")}, inc="dG", inc_by=16)
        def _(g, t=t):
            return g.dma_start(out=xnd.rearrange("(c p) (n q) -> p n c q", p=128, q=QT)[:, t, :, :],
                               in_=xn[:, :, t * QT:(t + 1) * QT])
        tv_tile_done[t] = prog.now("tV")
    P1 = prog.now("tV")
    P1d = prog.now("dG")
    esP.close()
    esP = ExitStack()
    sbp = lambda name, shape, dt: esP.enter_context(nc.sbuf_tensor(name, shape, dt))
    wq3 = sbp("wq3", [128, 2, 3, DCH, 128], BF16)
    qkp = sbp("qkp", [128, TPC], BF16)
    kkp = sbp("kkp", [128, SEQ], BF16)
    vkp = sbp("vkp", [128, SEQ], BF16)
    vkm = sbp("vkm", [128, SEQ // 128, 128], BF16)

    xn_tiles = xnd.rearrange("(c p) (o q) -> o p c q", p=128, q=QT)
    _dyn = {}

    @E("sync")
    def _(s):
        _dyn["o"] = [s.snap(s.partition_id() % 2 * 4 + t, min_val=0, max_val=7)
                     for t in range(NQT)]
        return None

    def dyn_own_tile(s, dstbuf, t):
        return s.dma_start(out=xnq[:, dstbuf, :, :], in_=xn_tiles[_dyn["o"][t], :, :, :])

    for hp in range(NH // 2):
        wbuf = hp % 2
        @E("gpsimd", wait={"tT": prog.now("tT")} if hp >= 2 else {"tV": P1}, inc="dG", inc_by=16)
        def _(g, hp=hp, wbuf=wbuf):
            return g.dma_start(
                out=wq3[:, wbuf, :, :, :],
                in_=wqkv.rearrange("i (c p) d -> p i c d", p=128)[:, :, :, hp * 128:(hp + 1) * 128])
        dG_w = prog.now("dG")

        for t in range(NQT):
            qb = t % 2
            @E("sync", wait={"tT": prog.now("tT"), "tV": P1, "dG": P1d}, inc="dS", inc_by=16)
            def _(s, t=t, qb=qb):
                return dyn_own_tile(s, qb, t)
            for dc in range(DCH):
                w = {"dS": prog.now("dS"), "dG": dG_w, "tV": prog.now("tV")} if dc == 0 else None
                @E("tensor", wait=w, inc="tT")
                def _(tn, dc=dc, t=t, qb=qb, wbuf=wbuf):
                    return tn.matmul(p_mm[:], wq3[:, wbuf, 0, dc, :], xnq[:, qb, dc, :],
                                     start=(dc == 0), stop=(dc == DCH - 1))
            @E("vector", wait={"tT": prog.now("tT")}, inc="tV")
            def _(v, t=t, hp=hp):
                return v.tensor_scalar(qkp[:, t * QT:(t + 1) * QT], p_mm[:],
                                       c_bqkv[:, 0, hp:hp + 1], None, ALU.add)
        for t in range(SEQ // QT):
            for dc in range(DCH):
                w = {"dG": dG_w, "tV": prog.now("tV")} if dc == 0 else None
                @E("tensor", wait=w, inc="tT")
                def _(tn, dc=dc, t=t, wbuf=wbuf):
                    return tn.matmul(p_mm[:], wq3[:, wbuf, 1, dc, :],
                                     xn[:, dc, t * QT:(t + 1) * QT],
                                     start=(dc == 0), stop=(dc == DCH - 1))
            for dc in range(DCH):
                @E("tensor", inc="tT")
                def _(tn, dc=dc, t=t, wbuf=wbuf):
                    return tn.matmul(p_mm2[:], wq3[:, wbuf, 2, dc, :],
                                     xn[:, dc, t * QT:(t + 1) * QT],
                                     start=(dc == 0), stop=(dc == DCH - 1))
            @E("vector", wait={"tT": prog.now("tT")}, inc="tV")
            def _(v, t=t, hp=hp):
                return v.tensor_scalar(kkp[:, t * QT:(t + 1) * QT], p_mm[:],
                                       c_bqkv[:, 1, hp:hp + 1], None, ALU.add)
            @E("vector", inc="tV")
            def _(v, t=t, hp=hp):
                return v.tensor_scalar(vkp[:, t * QT:(t + 1) * QT], p_mm2[:],
                                       c_bqkv[:, 2, hp:hp + 1], None, ALU.add)
        for kb in range(SEQ // 128):
            @E("tensor", wait={"tV": prog.now("tV")}, inc="tT")
            def _(tn, kb=kb):
                return tn.transpose(pt_b4[:], vkp[:, kb * 128:(kb + 1) * 128], c_iden[:])
            @E("vector", wait={"tT": prog.now("tT")}, inc="tV")
            def _(v, kb=kb):
                return v.tensor_copy(vkm[:, kb, :], pt_b4[:])
        tV_vkm = prog.now("tV")

        for half in range(2):
            h = hp * 2 + half
            rb = half * 64
            for kc in range(SEQ // 128):
                w = {"tV": tV_vkm} if kc == 0 else None
                @E("tensor", wait=w, inc="tT")
                def _(tn, kc=kc, rb=rb):
                    return tn.matmul(p_pv[rb:rb + 64, 0:1], vkm[:, kc, rb:rb + 64], onesb[:],
                                     start=(kc == 0), stop=(kc == SEQ // 128 - 1),
                                     tile_position=(0, rb))
            @E("vector", wait={"tT": prog.now("tT")}, inc="tV")
            def _(v, rb=rb):
                return v.tensor_copy(vsum[rb:rb + 64, :], p_pv[rb:rb + 64, 0:1])
            for j in range(NQT):
                NKC = SEQ // 128
                for kc in range(NKC):
                    pp = p_mm if kc % 2 == 0 else p_mm2
                    w = {"tC": prog.now("tC")} if kc >= 2 else {"tV": prog.now("tV")}
                    @E("tensor", wait=w, inc="tT")
                    def _(tn, kc=kc, rb=rb, j=j, pp=pp):
                        return tn.matmul(pp[:], kkp[rb:rb + 64, kc * 128:(kc + 1) * 128],
                                         qkp[rb:rb + 64, j * QT:(j + 1) * QT],
                                         start=True, stop=True)
                    @E("scalar", wait={"tT": prog.now("tT")}, inc="tC")
                    def _(s, kc=kc, pp=pp):
                        return s.activation(e1b[:, kc, :], pp[:], AF.Exp, scale=0.125)
                tC_exp = prog.now("tC")
                for kc in range(NKC):
                    w = {"tC": tC_exp} if kc == 0 else None
                    @E("tensor", wait=w, inc="tT")
                    def _(tn, kc=kc):
                        return tn.matmul(p_z[:], onesb[:], e1b[:, kc, :],
                                         start=(kc == 0), stop=(kc == NKC - 1))
                @E("vector", wait={"tT": prog.now("tT")}, inc="tV")
                def _(v):
                    return v.reciprocal(zrow[:], p_z[:])
                @E("tensor", wait={"tV": prog.now("tV")}, inc="tT")
                def _(tn):
                    return tn.matmul(p_bc2[:], onesf[:], zrow[:], start=True, stop=True)
                @E("vector", wait={"tT": prog.now("tT")}, inc="tV")
                def _(v):
                    return v.tensor_copy(rzb[:], p_bc2[:])
                tV_rz = prog.now("tV")
                for kc in range(NKC):
                    @E("vector", wait={"tV": tV_rz} if kc == 0 else None, inc="tV")
                    def _(v, kc=kc):
                        return v.tensor_mul(ptile[:], e1b[:, kc, :], rzb[:])
                    @E("scalar", wait={"tV": prog.now("tV")}, inc="tC")
                    def _(s, kc=kc):
                        return s.activation(ptile[:], ptile[:], AF.Exp)
                    @E("vector", wait={"tC": prog.now("tC")}, inc="tV")
                    def _(v, kc=kc):
                        return v.tensor_scalar_add(m2b[:, kc, :], ptile[:], -1.0)
                tV_m2 = prog.now("tV")
                for kc in range(NKC):
                    w = {"tV": tV_m2} if kc == 0 else None
                    @E("tensor", wait=w, inc="tT")
                    def _(tn, kc=kc):
                        return tn.matmul(p_z[:], onesb[:], m2b[:, kc, :],
                                         start=(kc == 0), stop=(kc == NKC - 1))
                @E("vector", wait={"tT": prog.now("tT")}, inc="tV")
                def _(v):
                    return v.tensor_scalar_add(zrow[:], p_z[:], float(SEQ))
                @E("vector", inc="tV")
                def _(v):
                    return v.reciprocal(zrow[:], zrow[:])
                @E("tensor", wait={"tV": prog.now("tV")}, inc="tT")
                def _(tn):
                    return tn.matmul(p_bc2[:], onesf[:], zrow[:], start=True, stop=True)
                @E("vector", wait={"tT": prog.now("tT")}, inc="tV")
                def _(v):
                    return v.tensor_copy(rzb[:], p_bc2[:])
                tV_rz2 = prog.now("tV")
                for kc in range(NKC):
                    w = {"tV": tV_rz2} if kc == 0 else None
                    @E("tensor", wait=w, inc="tT")
                    def _(tn, kc=kc, rb=rb):
                        return tn.matmul(p_pv[rb:rb + 64, :], vkm[:, kc, rb:rb + 64],
                                         m2b[:, kc, :], start=(kc == 0), stop=(kc == NKC - 1),
                                         tile_position=(0, rb))
                @E("vector", wait={"tT": prog.now("tT")}, inc="tV")
                def _(v, rb=rb, hp=hp, j=j):
                    return v.scalar_tensor_tensor(attn[rb:rb + 64, hp, j * QT:(j + 1) * QT],
                                                  p_pv[rb:rb + 64, :], vsum[rb:rb + 64, :],
                                                  rzb[rb:rb + 64, :], ALU.add, ALU.mult)
    P2 = prog.now("tV")
    esP.close()
    esP = ExitStack()
    sbp = lambda name, shape, dt: esP.enter_context(nc.sbuf_tensor(name, shape, dt))
    wosb = sbp("wosb", [128, DCH, D], BF16)
    x2til = sbp("x2til", [128, DCH, QT], F32)

    @E("gpsimd", wait={"tV": P2}, inc="dG", inc_by=16)
    def _(g):
        return g.dma_start(out=wosb[:, :, :], in_=wo.rearrange("(c p) d -> p c d", p=128))
    dG_wo = prog.now("dG")
    x2_tiles = x2b.rearrange("d (o q) -> o d q", q=QT)
    for j in range(NQT):
        @E("sync", wait={"tV": prog.now("tV"), "dO": prog.now("dO")}, inc="dS", inc_by=16)
        def _(s, j=j):
            return s.dma_start(out=x2til[:, :, :],
                               in_=x2_tiles.rearrange("o (c p) q -> o p c q", p=128)[_dyn["o"][j], :, :, :])
        dS_j = prog.now("dS")
        for dc in range(DCH):
            for hc in range(DCH):
                w = {"dG": dG_wo, "dS": dS_j, "tV": prog.now("tV")} if hc == 0 else None
                @E("tensor", wait=w, inc="tT")
                def _(tn, dc=dc, hc=hc, j=j):
                    return tn.matmul(p_mm[:], wosb[:, hc, dc * 128:(dc + 1) * 128],
                                     attn[:, hc, j * QT:(j + 1) * QT],
                                     start=(hc == 0), stop=(hc == DCH - 1))
            @E("vector", wait={"tT": prog.now("tT")}, inc="tV")
            def _(v, dc=dc):
                return v.scalar_tensor_tensor(x2til[:, dc, :], p_mm[:], c_bo[:, dc:dc + 1],
                                              x2til[:, dc, :], ALU.add, ALU.add)
        @E("gpsimd", wait={"tV": prog.now("tV")}, inc="dO", inc_by=16)
        def _(g, j=j):
            return g.dma_start(out=x3o.rearrange("(c p) (n q) -> p n c q", p=128, q=QT)[:, j, :, :],
                               in_=x2til[:, :, :])
    @E("gpsimd")
    def _(g):
        g.wait_ge(prog.sem["dO"], prog.now("dO"))
        return None

    esP.close()
    block = es.enter_context(nc.Block())
    prog.build(block)
    es.close()
    return nc


def prep_launch4_inputs(l3_x2, inp):
    import ml_dtypes
    wq = np.asarray(inp["L1_attn_wq"], np.float32)
    wk = np.asarray(inp["L1_attn_wk"], np.float32)
    wv = np.asarray(inp["L1_attn_wv"], np.float32)
    wo = np.asarray(inp["L1_attn_wo"], np.float32)
    wqkv = np.stack([np.ascontiguousarray(w.T) for w in (wq, wk, wv)])
    bq = np.asarray(inp["L1_attn_bq"], np.float32)
    bk = np.asarray(inp["L1_attn_bk"], np.float32)
    bv = np.asarray(inp["L1_attn_bv"], np.float32)
    bo = np.asarray(inp["L1_attn_bo"], np.float32)
    bqkv = np.stack([b.reshape(NH // 2, 128).T for b in (bq, bk, bv)], 1)
    bqkv = np.ascontiguousarray(bqkv)
    iden = np.eye(128, dtype=ml_dtypes.bfloat16)
    pp = lambda v: np.ascontiguousarray(v.reshape(DCH, 128).T)
    maps = []
    for c in range(NCORES):
        b = c // 2
        x2full = np.concatenate([np.asarray(l3_x2[2 * b]), np.asarray(l3_x2[2 * b + 1])], 1)
        maps.append({
            "x2b": np.ascontiguousarray(x2full),
            "wqkv": wqkv, "wo": np.ascontiguousarray(wo.T),
            "bqkv": bqkv, "bo": pp(bo), "iden": iden,
        })
    return maps

def build_launch5(dbg=False):
    nc = bass.Bass("TRN2", target_bir_lowering=False, debug=False)
    x3i = nc.dram_tensor("x3i", [D, TPC], F32, kind="ExternalInput").ap()
    w1c = nc.dram_tensor("w1c", [32, D, 128], F32, kind="ExternalInput").ap()
    w2c = nc.dram_tensor("w2c", [32, D, 128], F32, kind="ExternalInput").ap()
    h1c = nc.dram_tensor("h1c", [32, 32, 128, 128], F32, kind="ExternalInput").ap()
    h2c = nc.dram_tensor("h2c", [32, 32, 128, 128], F32, kind="ExternalInput").ap()
    w3c = nc.dram_tensor("w3c", [8, 32, 128, 128], F32, kind="ExternalInput").ap()
    s1 = nc.dram_tensor("s1", [128, 32], F32, kind="ExternalInput").ap()
    s2 = nc.dram_tensor("s2", [128, 32], F32, kind="ExternalInput").ap()
    sh1 = nc.dram_tensor("sh1", [128, 32], F32, kind="ExternalInput").ap()
    sh2 = nc.dram_tensor("sh2", [128, 32], F32, kind="ExternalInput").ap()
    s3 = nc.dram_tensor("s3", [128, DCH], F32, kind="ExternalInput").ap()
    x4o = nc.dram_tensor("x4o", [D, TPC], F32, kind="ExternalOutput").ap()

    es = ExitStack()
    sb = lambda name, shape, dt: es.enter_context(nc.sbuf_tensor(name, shape, dt))
    psu = lambda name, shape, dt=F32: es.enter_context(nc.psum_tensor(name, shape, dt))

    TP = 512
    xn4 = sb("xn4", [128, DCH, TPC], BF16)
    hA = sb("hA", [128, 32, TP], BF16)
    hB = sb("hB", [128, 32, TP], BF16)
    c_s1 = sb("c_s1", [128, 32], F32)
    c_s2 = sb("c_s2", [128, 32], F32)
    c_sh1 = sb("c_sh1", [128, 32], F32)
    c_sh2 = sb("c_sh2", [128, 32], F32)
    c_s3 = sb("c_s3", [128, DCH], F32)
    onesb = sb("onesb", [128, 1], BF16)
    onesf = sb("onesf", [1, 128], F32)
    murow = sb("murow", [1, QT], F32)
    rsrow = sb("rsrow", [1, QT], F32)
    tmprow = sb("tmprow", [1, QT], F32)
    mub = sb("mub", [128, QT], F32)
    rstdb = sb("rstdb", [128, QT], F32)
    esP = ExitStack()
    sbp = lambda name, shape, dt: esP.enter_context(nc.sbuf_tensor(name, shape, dt))
    xf = sbp("xf", [128, DCH, QT], F32)
    xsq = sbp("xsq", [128, DCH, QT], BF16)
    xb16 = sbp("xb16", [128, DCH, QT], BF16)

    p_a = psu("p_a", [128, 512])
    p_b = psu("p_b", [128, 512])
    p_s1 = psu("p_s1", [1, QT])
    p_s2 = psu("p_s2", [1, QT])
    p_bc = psu("p_bc", [128, QT])
    p_bc2 = psu("p_bc2", [128, QT])

    prog = Prog(nc, es)
    E = prog.emit

    for dst, srcv in [(c_s1, s1[:, :]), (c_s2, s2[:, :]), (c_sh1, sh1[:, :]),
                      (c_sh2, sh2[:, :]), (c_s3, s3[:, :])]:
        @E("sync", inc="dS", inc_by=16)
        def _(s_, dst=dst, srcv=srcv):
            return s_.dma_start(out=dst[:], in_=srcv)
    @E("vector", inc="tV")
    def _(v):
        return v.memset(onesb[:], 1.0)
    @E("vector", inc="tV")
    def _(v):
        return v.memset(onesf[:], 1.0)
    CONSTS = prog.now("dS", "tV")

    x3_tiles = x3i.rearrange("(c p) (n q) -> p n c q", p=128, q=QT)
    for t in range(NQT):
        w = {"tV": prog.now("tV"), "tC": prog.now("tC"), "tT": prog.now("tT")} if t else {}
        @E("sync", wait=w, inc="dS", inc_by=16)
        def _(s_, t=t):
            return s_.dma_start(out=xf[:, :, :], in_=x3_tiles[:, t, :, :])
        @E("vector", wait={**({} if t else CONSTS), "dS": prog.now("dS")}, inc="tV")
        def _(v):
            return v.tensor_copy(xb16[:].rearrange("p c q -> p (c q)"),
                                 xf[:].rearrange("p c q -> p (c q)"))
        @E("scalar", wait={"tV": prog.now("tV")}, inc="tC")
        def _(s_):
            return s_.activation(xsq[:].rearrange("p c q -> p (c q)"),
                                 xb16[:].rearrange("p c q -> p (c q)"), AF.Square)
        w_mm = {"tV": prog.now("tV"), "tC": prog.now("tC")}
        for dc in range(DCH):
            @E("tensor", wait=(w_mm if dc == 0 else None), inc="tT")
            def _(tn, dc=dc):
                return tn.matmul(p_s1[:], onesb[:], xb16[:, dc, :],
                                 start=(dc == 0), stop=(dc == DCH - 1))
        for dc in range(DCH):
            @E("tensor", inc="tT")
            def _(tn, dc=dc):
                return tn.matmul(p_s2[:], onesb[:], xsq[:, dc, :],
                                 start=(dc == 0), stop=(dc == DCH - 1))
        emit_ln_rows(prog, p_s1, p_s2, murow, rsrow, tmprow, D, 1e-5)
        @E("tensor", wait={"tV": prog.now("tV")}, inc="tT")
        def _(tn):
            return tn.matmul(p_bc[:], onesf[:], murow[:], start=True, stop=True)
        @E("tensor", inc="tT")
        def _(tn):
            return tn.matmul(p_bc2[:], onesf[:], rsrow[:], start=True, stop=True)
        @E("vector", wait={"tT": prog.now("tT")}, inc="tV")
        def _(v):
            return v.tensor_copy(mub[:], p_bc[:])
        @E("vector", inc="tV")
        def _(v):
            return v.tensor_copy(rstdb[:], p_bc2[:])
        for dc in range(DCH):
            @E("vector", wait={"tV": prog.now("tV")} if dc == 0 else None, inc="tV")
            def _(v, dc=dc):
                return v.tensor_sub(xf[:, dc, :], xf[:, dc, :], mub[:])
        for dc in range(DCH):
            @E("vector", inc="tV")
            def _(v, dc=dc, t=t):
                return v.tensor_mul(xn4[:, dc, t * QT:(t + 1) * QT], xf[:, dc, :], rstdb[:])
    P1 = prog.now("tV")
    if dbg:
        dbg_xn4 = nc.dram_tensor("dbg_xn4", [128, DCH * TPC], BF16, kind="ExternalOutput").ap()
        @E("gpsimd", wait={"tV": P1}, inc="dO", inc_by=16)
        def _(g):
            return g.dma_start(out=dbg_xn4[:, :], in_=xn4[:].rearrange("p c q -> p (c q)"))
    esP.close()
    esP = ExitStack()
    sbp = lambda name, shape, dt: esP.enter_context(nc.sbuf_tensor(name, shape, dt))
    w12sb = sbp("w12sb", [128, 3, 2, DCH, 128], BF16)
    hwsb = sbp("hwsb", [128, 3, 32, 128], BF16)
    gsl = sbp("gsl", [128, 512], F32)
    x3til = sbp("x3til", [128, QT], F32)

    for tp in range(TPC // TP):
        for fc in range(32):
            wb = fc % 3
            @E("gpsimd", wait={"tT": prog.now("tT"), "tV": P1}, inc="dG", inc_by=16)
            def _(g, fc=fc, wb=wb):
                return g.dma_start(out=w12sb[:, wb, 0, :, :],
                                   in_=w1c.rearrange("f (c p) w -> f p c w", p=128)[fc, :, :, :])
            @E("gpsimd", inc="dG", inc_by=16)
            def _(g, fc=fc, wb=wb):
                return g.dma_start(out=w12sb[:, wb, 1, :, :],
                                   in_=w2c.rearrange("f (c p) w -> f p c w", p=128)[fc, :, :, :])
            dG_w = prog.now("dG")
            for sub in range(TP // 512):
                for dc in range(DCH):
                    w = {"dG": dG_w, "tV": prog.now("tV"), "tC": prog.now("tC")} if dc == 0 else None
                    @E("tensor", wait=w, inc="tT")
                    def _(tn, fc=fc, wb=wb, dc=dc, sub=sub, tp=tp):
                        return tn.matmul(p_a[:], w12sb[:, wb, 0, dc, :],
                                         xn4[:, dc, tp * TP + sub * 512:tp * TP + (sub + 1) * 512],
                                         start=(dc == 0), stop=(dc == DCH - 1))
                for dc in range(DCH):
                    @E("tensor", inc="tT")
                    def _(tn, fc=fc, wb=wb, dc=dc, sub=sub, tp=tp):
                        return tn.matmul(p_b[:], w12sb[:, wb, 1, dc, :],
                                         xn4[:, dc, tp * TP + sub * 512:tp * TP + (sub + 1) * 512],
                                         start=(dc == 0), stop=(dc == DCH - 1))
                @E("scalar", wait={"tT": prog.now("tT"), "tV": prog.now("tV")}, inc="tC")
                def _(s_, fc=fc):
                    return s_.activation(gsl[:], p_a[:], AF.Silu, scale=c_s1[:, fc:fc + 1])
                @E("vector", wait={"tC": prog.now("tC")}, inc="tV")
                def _(v, fc=fc, sub=sub):
                    return v.scalar_tensor_tensor(hA[:, fc, sub * 512:(sub + 1) * 512],
                                                  p_b[:], c_s2[:, fc:fc + 1], gsl[:],
                                                  ALU.mult, ALU.mult)
        if dbg and tp == 0:
            dbg_hA = nc.dram_tensor("dbg_hA", [128, 32 * TP], BF16, kind="ExternalOutput").ap()
            @E("gpsimd", wait={"tV": prog.now("tV")}, inc="dO", inc_by=16)
            def _(g):
                return g.dma_start(out=dbg_hA[:, :], in_=hA[:].rearrange("p c q -> p (c q)"))
        for oc in range(32):
            wb = oc % 3
            @E("gpsimd", wait={"tT": prog.now("tT")}, inc="dG", inc_by=16)
            def _(g, oc=oc, wb=wb):
                return g.dma_start(out=hwsb[:, wb, :, :], in_=h1c[oc, :, :, :].rearrange("ic p po -> p ic po"))
            dG_w = prog.now("dG")
            for sub in range(TP // 512):
                for ic in range(32):
                    w = {"dG": dG_w, "tV": prog.now("tV"), "tC": prog.now("tC")} if ic == 0 else None
                    @E("tensor", wait=w, inc="tT")
                    def _(tn, oc=oc, wb=wb, ic=ic, sub=sub):
                        return tn.matmul(p_a[:], hwsb[:, wb, ic, :],
                                         hA[:, ic, sub * 512:(sub + 1) * 512],
                                         start=(ic == 0), stop=(ic == 31))
                @E("scalar", wait={"tT": prog.now("tT")}, inc="tC")
                def _(s_, oc=oc, sub=sub):
                    return s_.activation(hB[:, oc, sub * 512:(sub + 1) * 512], p_a[:],
                                         AF.Gelu, scale=c_sh1[:, oc:oc + 1])
        if dbg and tp == 0:
            dbg_hB = nc.dram_tensor("dbg_hB", [128, 32 * TP], BF16, kind="ExternalOutput").ap()
            @E("gpsimd", wait={"tC": prog.now("tC")}, inc="dO", inc_by=16)
            def _(g):
                return g.dma_start(out=dbg_hB[:, :], in_=hB[:].rearrange("p c q -> p (c q)"))
        for oc in range(32):
            wb = oc % 3
            @E("gpsimd", wait={"tT": prog.now("tT")}, inc="dG", inc_by=16)
            def _(g, oc=oc, wb=wb):
                return g.dma_start(out=hwsb[:, wb, :, :], in_=h2c[oc, :, :, :].rearrange("ic p po -> p ic po"))
            dG_w = prog.now("dG")
            for sub in range(TP // 512):
                for ic in range(32):
                    w = {"dG": dG_w, "tV": prog.now("tV"), "tC": prog.now("tC")} if ic == 0 else None
                    @E("tensor", wait=w, inc="tT")
                    def _(tn, oc=oc, wb=wb, ic=ic, sub=sub):
                        return tn.matmul(p_a[:], hwsb[:, wb, ic, :],
                                         hB[:, ic, sub * 512:(sub + 1) * 512],
                                         start=(ic == 0), stop=(ic == 31))
                @E("scalar", wait={"tT": prog.now("tT")}, inc="tC")
                def _(s_, oc=oc, sub=sub):
                    return s_.activation(hA[:, oc, sub * 512:(sub + 1) * 512], p_a[:],
                                         AF.Gelu, scale=c_sh2[:, oc:oc + 1])
        for dc in range(DCH):
            wb = dc % 3
            @E("gpsimd", wait={"tT": prog.now("tT")}, inc="dG", inc_by=16)
            def _(g, dc=dc, wb=wb):
                return g.dma_start(out=hwsb[:, wb, :, :], in_=w3c[dc, :, :, :].rearrange("fc p pd -> p fc pd"))
            dG_w = prog.now("dG")
            for sub in range(TP // 512):
                @E("sync", wait={"dO": prog.now("dO")}, inc="dS", inc_by=16)
                def _(s_, tp=tp, sub=sub, dc=dc):
                    return s_.dma_start(out=x3til[:, :],
                                        in_=x3_tiles[:, tp * (TP // QT) + sub, dc, :])
                dS_x = prog.now("dS")
                for fc in range(32):
                    w = {"dG": dG_w, "dS": dS_x, "tV": prog.now("tV"),
                         "tC": prog.now("tC")} if fc == 0 else None
                    @E("tensor", wait=w, inc="tT")
                    def _(tn, dc=dc, wb=wb, fc=fc, sub=sub):
                        return tn.matmul(p_a[:], hwsb[:, wb, fc, :],
                                         hA[:, fc, sub * 512:(sub + 1) * 512],
                                         start=(fc == 0), stop=(fc == 31))
                @E("vector", wait={"tT": prog.now("tT")}, inc="tV")
                def _(v, dc=dc, sub=sub):
                    return v.scalar_tensor_tensor(x3til[:, :], p_a[:], c_s3[:, dc:dc + 1],
                                                  x3til[:, :], ALU.mult, ALU.add)
                @E("gpsimd", wait={"tV": prog.now("tV")}, inc="dO", inc_by=16)
                def _(g, tp=tp, sub=sub, dc=dc):
                    return g.dma_start(
                        out=x4o.rearrange("(c p) (n q) -> p n c q", p=128, q=QT)[:, tp * (TP // QT) + sub, dc, :],
                        in_=x3til[:, :])
    @E("gpsimd")
    def _(g):
        g.wait_ge(prog.sem["dO"], prog.now("dO"))
        return None

    esP.close()
    block = es.enter_context(nc.Block())
    prog.build(block)
    es.close()
    return nc


def prep_launch5_inputs(l4_x3, inp):
    w1 = np.asarray(inp["L1_ffn_w1_w"], np.float32)
    w2 = np.asarray(inp["L1_ffn_w2_w"], np.float32)
    h1 = np.asarray(inp["L1_ffn_h1_w"], np.float32)
    h2 = np.asarray(inp["L1_ffn_h2_w"], np.float32)
    w3 = np.asarray(inp["L1_ffn_w3_w"], np.float32)
    ppf = lambda v, n: np.ascontiguousarray(v.reshape(n, 128).T)
    w1c = np.ascontiguousarray(w1.T.reshape(D, 32, 128).transpose(1, 0, 2))
    w2c = np.ascontiguousarray(w2.T.reshape(D, 32, 128).transpose(1, 0, 2))
    h1c = np.ascontiguousarray(h1.reshape(32, 128, 32, 128).transpose(0, 2, 3, 1))
    h2c = np.ascontiguousarray(h2.reshape(32, 128, 32, 128).transpose(0, 2, 3, 1))
    w3cc = np.ascontiguousarray(w3.reshape(8, 128, 32, 128).transpose(0, 2, 3, 1))
    com = {
        "w1c": w1c, "w2c": w2c, "h1c": h1c, "h2c": h2c, "w3c": w3cc,
        "s1": ppf(np.asarray(inp["L1_ffn_w1_s"], np.float32), 32),
        "s2": ppf(np.asarray(inp["L1_ffn_w2_s"], np.float32), 32),
        "sh1": ppf(np.asarray(inp["L1_ffn_h1_s"], np.float32), 32),
        "sh2": ppf(np.asarray(inp["L1_ffn_h2_s"], np.float32), 32),
        "s3": ppf(np.asarray(inp["L1_ffn_w3_s"], np.float32), DCH),
    }
    return [{**com, "x3i": np.asarray(l4_x3[c])} for c in range(NCORES)]

_PROGS = {}


def kernel(**inputs):
    inp = {k: np.asarray(v) for k, v in inputs.items()}
    if "p" not in _PROGS:
        _PROGS["p"] = (build_launch1(), build_launch2(), build_launch3(),
                       build_launch4(), build_launch5())
    nc1, nc2, nc3, nc4, nc5 = _PROGS["p"]
    cores = list(range(NCORES))
    r1 = run_bass_kernel_spmd(nc1, prep_launch1_inputs(inp), core_ids=cores).results
    r2 = run_bass_kernel_spmd(nc2, prep_launch2_inputs(r1, inp), core_ids=cores).results
    r3 = run_bass_kernel_spmd(nc3, prep_launch3_inputs(r1, r2), core_ids=cores).results
    x2s = [r3[c]["x2o"] for c in range(NCORES)]
    r4 = run_bass_kernel_spmd(nc4, prep_launch4_inputs(x2s, inp), core_ids=cores).results
    x3s = [r4[c]["x3o"] for c in range(NCORES)]
    r5 = run_bass_kernel_spmd(nc5, prep_launch5_inputs(x3s, inp), core_ids=cores).results
    out = np.zeros((4, SEQ, D), np.float32)
    for c in range(NCORES):
        b, half = c // 2, c % 2
        out[b, half * TPC:(half + 1) * TPC, :] = np.asarray(r5[c]["x4o"]).T
    return out
